# version 13
# baseline (speedup 1.0000x reference)
import math
import numpy as np
import ml_dtypes
import concourse.bass as bass
import concourse.mybir as mybir
from concourse.bass import IndirectOffsetOnAxis
from concourse.bass_utils import run_bass_kernel_spmd
from contextlib import ExitStack

F32 = mybir.dt.float32
BF16 = mybir.dt.bfloat16
I32 = mybir.dt.int32
UI32 = mybir.dt.uint32
AF = mybir.ActivationFunctionType
OP = mybir.AluOpType
AX = mybir.AxisListType

D = 4096
IDIM = 11008
NF = 2048
GK = 32
GI = 86
EPS = 1e-5
GUW = IDIM // 8
DW = D // 8
GU_SLICES = [(0, 2816), (2816, 2816), (5632, 2816), (8448, 2560)]
DN_SLICES = [(0, 2048), (2048, 2048)]
POS_OF_SHIFT = [0, 2, 4, 6, 1, 3, 5, 7]


def _bf(x):
    return np.asarray(x, dtype=ml_dtypes.bfloat16)


_CONST = {}


def _consts():
    if not _CONST:
        t = np.arange(D, dtype=np.int64)
        f = np.arange(1, NF + 1, dtype=np.int64)
        ph = ((t[:, None] * f[None, :]) % D).astype(np.float64) * (2 * math.pi / D)
        fc4 = _bf(np.cos(ph)).reshape(32, 128, 16, 128).transpose(2, 1, 0, 3).copy()
        fs4 = _bf(np.sin(ph)).reshape(32, 128, 16, 128).transpose(2, 1, 0, 3).copy()
        ll = np.arange(D, dtype=np.int64)
        phc = ((f[:, None] * ll[None, :]) % D).astype(np.float64) * (2 * math.pi / D)
        cc = np.cos(phc) * (2.0 / D)
        cc[NF - 1, :] *= 0.5
        g32 = (np.arange(D)[:, None] // 128 == np.arange(GK)[None, :]).astype(np.float32)
        g86 = (np.arange(IDIM)[:, None] // 128 == np.arange(GI)[None, :]).astype(np.float32)
        _CONST.update(
            fc4=fc4, fs4=fs4, ccos=_bf(cc).copy(),
            g32=_bf(g32).reshape(GK, 128, GK).transpose(1, 0, 2).copy(),
            g86=_bf(g86).reshape(GI, 128, GI).transpose(1, 0, 2).copy())
    return _CONST


class Sem:
    __slots__ = ("h", "n")

    def __init__(self, ctx, nc, name):
        self.h = ctx.enter_context(nc.semaphore(name))
        self.n = 0

    def inc(self, instr, k=1):
        instr.then_inc(self.h, k)
        self.n += k
        return self.n


def build_program(R):
    NRT = R // 128
    GRP = min(512, R)
    NG = R // GRP
    NRTG = GRP // 128
    TCS = min(512, R)
    NTC = R // TCS

    nc = bass.Bass("TRN2", target_bir_lowering=False)

    x_in = nc.dram_tensor("x", [R, D], F32, kind="ExternalInput").ap()
    out_d = nc.dram_tensor("out", [R, D], F32, kind="ExternalOutput").ap()
    gq = nc.dram_tensor("gq", [D, GUW], I32, kind="ExternalInput").ap()
    gs = nc.dram_tensor("gs", [GK, IDIM], F32, kind="ExternalInput").ap()
    gz = nc.dram_tensor("gz", [GK, GUW], I32, kind="ExternalInput").ap()
    uq = nc.dram_tensor("uq", [D, GUW], I32, kind="ExternalInput").ap()
    us = nc.dram_tensor("us", [GK, IDIM], F32, kind="ExternalInput").ap()
    uz = nc.dram_tensor("uz", [GK, GUW], I32, kind="ExternalInput").ap()
    dq = nc.dram_tensor("dq", [IDIM, DW], I32, kind="ExternalInput").ap()
    ds = nc.dram_tensor("ds", [GI, D], F32, kind="ExternalInput").ap()
    dz = nc.dram_tensor("dz", [GI, DW], I32, kind="ExternalInput").ap()
    fc_d = nc.dram_tensor("fc4", [16, 128, 32, 128], BF16, kind="ExternalInput").ap()
    fs_d = nc.dram_tensor("fs4", [16, 128, 32, 128], BF16, kind="ExternalInput").ap()
    ccos_d = nc.dram_tensor("ccos", [NF, D], BF16, kind="ExternalInput").ap()
    g32_d = nc.dram_tensor("g32", [128, GK, GK], BF16, kind="ExternalInput").ap()
    g86_d = nc.dram_tensor("g86", [128, GI, GI], BF16, kind="ExternalInput").ap()

    zz_d = nc.dram_tensor("zz_s", [R, 2 * D], BF16, kind="Internal").ap()
    x1_d = nc.dram_tensor("x1_s", [R, D], F32, kind="Internal").ap()
    ln3_d = nc.dram_tensor("ln3_s", [R, D], BF16, kind="Internal").ap()
    wg_d = nc.dram_tensor("wg_s", [43, GK, 128, 256], BF16, kind="Internal").ap()
    wu_d = nc.dram_tensor("wu_s", [43, GK, 128, 256], BF16, kind="Internal").ap()
    wd_d = nc.dram_tensor("wd_s", [16, GI, 128, 256], BF16, kind="Internal").ap()
    h_d = nc.dram_tensor("h_s", [GI, 128, R], BF16, kind="Internal").ap()
    ffnT_d = nc.dram_tensor("ffnT_s", [D, R], BF16, kind="Internal").ap()
    scb_gu_d = nc.dram_tensor("scbgu_s", [64, IDIM], BF16, kind="Internal").ap()
    scb_dn_d = nc.dram_tensor("scbdn_s", [GI, D], BF16, kind="Internal").ap()

    ctx = ExitStack()
    E = ctx.enter_context
    SS = lambda name: Sem(ctx, nc, name)

    ps = [E(nc.psum_tensor(f"psb{i}", [128, 512], F32)) for i in range(8)]

    mv = E(nc.sbuf_tensor([128, 2], F32))
    bnst = E(nc.sbuf_tensor([128, 48], F32))
    std_t = E(nc.sbuf_tensor([128, 1], F32))
    rs_t = E(nc.sbuf_tensor([128, 1], F32))
    nmrs_t = E(nc.sbuf_tensor([128, 1], F32))
    stkv = E(nc.sbuf_tensor([128, 64 * 4], F32))
    stki = E(nc.sbuf_tensor([128, 64 * 4], F32))
    tmpi8 = E(nc.sbuf_tensor([128, 8], UI32))
    g8 = E(nc.sbuf_tensor([128, 8], F32))
    negmax = E(nc.sbuf_tensor([128, 1], F32))
    exp8 = E(nc.sbuf_tensor([128, 8], F32))
    sume = E(nc.sbuf_tensor([128, 1], F32))
    rsum = E(nc.sbuf_tensor([128, 1], F32))
    w8 = E(nc.sbuf_tensor([128, 8], F32))
    mask64 = E(nc.sbuf_tensor([128, 64], F32))
    d8f = E(nc.sbuf_tensor([128, 8], F32))
    idx_i = E(nc.sbuf_tensor([128, 8], I32))
    rowb_i = E(nc.sbuf_tensor([128, 1], I32))
    rowb_f = E(nc.sbuf_tensor([128, 1], F32))
    zs_dn = E(nc.sbuf_tensor([GI, D], BF16))
    eps_t = E(nc.sbuf_tensor([128, 1], F32))

    s_set = SS("set")
    s_dve = SS("dve")
    s_scl = SS("scl")
    s_gps = SS("gps")
    s_pe = SS("pe")
    s_x0 = SS("x0")
    s_x0g = SS("x0g")
    s_x0f = SS("x0f")
    s_zz = SS("zz")
    s_l1t = SS("l1t")
    s_fld = [SS(f"fld{i}") for i in range(3)]
    s_ffree = SS("ffree")
    s_cld = [SS(f"cld{i}") for i in range(2)]
    s_cuse = SS("cuse")
    s_cstop = SS("cstop")
    s_cread = SS("cread")
    s_pk = [SS(f"pk{i}") for i in range(2)]
    s_pkg = [SS(f"pkg{i}") for i in range(2)]
    s_scrg = [SS(f"scrg{i}") for i in range(2)]
    s_wwrg = [SS(f"wwrg{i}") for i in range(2)]
    s_scr = [SS(f"scr{i}") for i in range(2)]
    s_scw = SS("scw")
    s_wwr = [SS(f"wwr{i}") for i in range(2)]
    s_gat = [SS("gat0"), SS("gat1")]
    s_gatf = SS("gatf")
    s_x1w = [SS("x1w0"), SS("x1w1")]
    s_l3w = SS("l3w")
    s_l3t = SS("l3t")
    s_wgu = [SS(f"wgu{i}") for i in range(3)]
    s_wguf = SS("wguf")
    s_hw = [SS("hw0"), SS("hw1")]
    s_ffnfree = SS("ffnfree")
    s_hld = SS("hld")
    s_wdl = [SS(f"wdl{i}") for i in range(3)]
    s_wdf = SS("wdf")
    s_dfree = SS("dfree")
    s_fT = [SS("fT0"), SS("fT1")]
    s_fTl = [SS("fTl0"), SS("fTl1")]
    s_x1l = [SS("x1l0"), SS("x1l1")]
    s_p4f = SS("p4f")
    s_out = [SS("out0"), SS("out1")]

    blk = E(nc.Block())
    te, v, sc, gp, sy = nc.tensor, nc.vector, nc.scalar, nc.gpsimd, nc.sync

    s1 = ExitStack()
    S1 = s1.enter_context
    pk_sb = [S1(nc.sbuf_tensor(f"pk{i}", [128, 352], I32)) for i in range(2)]
    nib_sb = [S1(nc.sbuf_tensor(f"nib{i}", [128, 2816], mybir.dt.int16)) for i in range(2)]
    sbc_sb = [S1(nc.sbuf_tensor(f"sbc{i}", [128, 2816], BF16)) for i in range(2)]

    s_dve.inc(v.memset(eps_t[:], EPS))
    p0 = ExitStack()
    sc32_gu = p0.enter_context(nc.sbuf_tensor([64, IDIM], F32))
    sc32_dn = p0.enter_context(nc.sbuf_tensor([GI, D], F32))
    sc_gu = p0.enter_context(nc.sbuf_tensor([64, IDIM], BF16))
    sc_dn = p0.enter_context(nc.sbuf_tensor([GI, D], BF16))

    for dst, src in ((sc32_gu[0:GK, :], gs), (sc32_gu[GK:64, :], us),
                     (sc32_dn[:], ds)):
        s_set.inc(sy.dma_start(out=dst, in_=src[:]), 16)
    v.wait_ge(s_set.h, s_set.n)
    s_dve.inc(v.tensor_copy(sc_gu[:], sc32_gu[:]))
    s_dve.inc(v.tensor_copy(sc_dn[:], sc32_dn[:]))
    P0_DONE = s_dve.n
    sc.wait_ge(s_dve.h, P0_DONE)
    s_scw.inc(sc.dma_start(out=scb_gu_d[:], in_=sc_gu[:]), 16)
    s_scw.inc(sc.dma_start(out=scb_dn_d[:], in_=sc_dn[:]), 16)
    p0.close()

    def ln_stats(src_ap, fd=D):
        nch = max(1, fd // 512)
        cw = fd // nch
        v.wait_ge(s_dve.h, s_dve.n)
        for c in range(nch):
            s_dve.inc(v.bn_stats(bnst[:, 6 * c:6 * c + 6],
                                 src_ap[:, c * cw:(c + 1) * cw]))
        v.wait_ge(s_dve.h, s_dve.n)
        s_dve.inc(v.bn_aggr(mv[:], bnst[:, 0:6 * nch]))
        sc.wait_ge(s_dve.h, s_dve.n)
        s_scl.inc(sc.activation(std_t[:], mv[:, 1:2], AF.Sqrt, bias=eps_t[:], scale=1.0))
        v.wait_ge(s_scl.h, s_scl.n)
        s_dve.inc(v.reciprocal(rs_t[:], std_t[:]))
        v.wait_ge(s_dve.h, s_dve.n)
        s_dve.inc(v.scalar_tensor_tensor(nmrs_t[:], mv[:, 0:1], -1.0, rs_t[:],
                                         OP.mult, OP.mult))
        return s_dve.n

    DQ_UNITS = ([("g", kc, sl) for kc in range(GK) for sl in range(4)]
                + [("u", kc, sl) for kc in range(GK) for sl in range(4)]
                + [("d", ic, sl) for ic in range(GI) for sl in range(2)])
    N_GU = 256
    N_ALL = len(DQ_UNITS)
    dqs = {"u": 0, "i": 0, "mdone": [0, 0], "pkslot": 0,
           "pkf": [0, 0], "sbcslot": 0}

    def dequant_unit(which, kc, sl):
        u = dqs["u"]
        dqs["u"] += 1
        slot = u % 2
        col0, ncol = (GU_SLICES[sl] if which != "d" else DN_SLICES[sl])
        nw = ncol // 8
        if which == "g":
            src_q, wdst, scsrc = gq, wg_d, scb_gu_d[kc:kc + 1, :]
        elif which == "u":
            src_q, wdst, scsrc = uq, wu_d, scb_gu_d[GK + kc:GK + kc + 1, :]
        else:
            src_q, wdst, scsrc = dq, wd_d, scb_dn_d[kc:kc + 1, :]
        qe = gp if which == "d" else sy
        pk_sems = s_pkg if which == "d" else s_pk
        scr_sems = s_scrg if which == "d" else s_scr
        wwr_sems = s_wwrg if which == "d" else s_wwr
        pslot = dqs["pkslot"] % 2
        dqs["pkslot"] += 1
        if dqs["pkf"][pslot] > 0:
            qe.wait_ge(s_dve.h, dqs["pkf"][pslot])
        pk_sems[pslot].inc(qe.dma_start(
            out=pk_sb[pslot][:, 0:nw],
            in_=src_q[kc * 128:(kc + 1) * 128, col0 // 8:col0 // 8 + nw]), 16)
        dqs["pk_evt"] = (pk_sems[pslot].h, pk_sems[pslot].n)
        bslot = dqs["sbcslot"] % 2
        dqs["sbcslot"] += 1
        if u < 2:
            qe.wait_ge(s_scw.h, 32)
        if u >= 2:
            qe.wait_ge(s_dve.h, dqs.get("mdone_u%d" % (u - 2), 0))
            ws, wv = dqs["wwrval_u%d" % (u - 2)]
            qe.wait_ge(ws, wv)
        scr_sems[bslot].inc(qe.dma_start(
            out=sbc_sb[bslot][:, 0:ncol].rearrange("p (a c) -> p a c", a=1),
            in_=scsrc[:, col0:col0 + ncol].partition_broadcast(128)), 16)
        dqs["scr_evt"] = (scr_sems[bslot].h, scr_sems[bslot].n)
        v.wait_ge(*dqs["pk_evt"])
        if u >= 2:
            v.wait_ge(s_dve.h, dqs["mdone"][slot])
        pk16 = pk_sb[pslot][:, 0:nw].bitcast(mybir.dt.int16)
        nibv = nib_sb[slot][:, 0:8 * nw].rearrange("p (c h) -> p c h", h=8)
        for a in range(4):
            dst = nibv[:, :, 2 * a:2 * a + 2]
            s_dve.inc(v.tensor_scalar(
                dst, pk16[:, 0:2 * nw],
                4 * a, 15, OP.logical_shift_right, OP.bitwise_and))
        v.wait_ge(s_dve.h, s_dve.n)
        v.wait_ge(*dqs["scr_evt"])
        s_dve.inc(v.tensor_tensor(sbc_sb[bslot][:, 0:ncol], nib_sb[slot][:, 0:ncol],
                                  sbc_sb[bslot][:, 0:ncol], OP.mult))
        dqs["mdone"][slot] = s_dve.n
        dqs["mdone_u%d" % u] = s_dve.n
        dqs["pkf"][pslot] = s_dve.n
        qw = gp if which == "d" else sc
        qw.wait_ge(s_dve.h, s_dve.n)
        nblk = ncol // 256
        blk0 = col0 // 256
        dst = wdst[blk0:blk0 + nblk, kc].rearrange("b p c -> p b c")
        srcv = sbc_sb[bslot][:, 0:ncol].rearrange("p (b c) -> p b c", c=256)
        wwr_sems[bslot].inc(qw.dma_start(out=dst, in_=srcv), 16)
        dqs["wwrval_u%d" % u] = (wwr_sems[bslot].h, wwr_sems[bslot].n)
        if u == N_GU - 1:
            dqs["gu_wwr_totals"] = [s.n for s in s_wwr]

    def emit_dequant(n):
        i0 = dqs["i"]
        for i in range(i0, min(i0 + n, N_ALL)):
            dequant_unit(*DQ_UNITS[i])
        dqs["i"] = min(i0 + n, N_ALL)

    aph = ExitStack()
    AE = aph.enter_context
    ln1T = AE(nc.sbuf_tensor([128, 32 * GRP], BF16))
    PT = AE(nc.sbuf_tensor([128, 16 * GRP], BF16))
    x0_t = AE(nc.sbuf_tensor([128, D], F32))

    x0_loads = 0
    evac_done = [0, 0, 0, 0]
    fld_val = {}
    gat_n = 0
    gat_free = [0, 0]
    rowb_used = [0]
    x1_writes = 0
    l3_writes = 0
    x0_free_evt = None
    last_evac = 0
    fpair_done = {}
    cuse_done = {}
    cstop_done = {}
    cread_done = {}

    ga = ExitStack()
    ln1b_t = ga.enter_context(nc.sbuf_tensor([128, D], BF16))
    for rt in range(NRT):
        r0 = rt * 128
        if x0_free_evt is None:
            sy.wait_ge(s_dve.h, P0_DONE)
            sy.wait_ge(s_scw.h, 32)
        else:
            sy.wait_ge(x0_free_evt[0], x0_free_evt[1])
        s_x0.inc(sy.dma_start(out=x0_t[:], in_=x_in[r0:r0 + 128, :]), 16)
        x0_loads += 1
        v.wait_ge(s_x0.h, s_x0.n)
        ln_stats(x0_t)
        sc.wait_ge(s_dve.h, s_dve.n)
        if rt > 0:
            sc.wait_ge(s_zz.h, s_zz.n)
        ins = sc.activation(ln1b_t[:], x0_t[:], AF.Identity,
                            bias=nmrs_t[:], scale=rs_t[:])
        s_scl.inc(ins)
        x0_free_evt = (s_scl.h, s_scl.n)
        sc.wait_ge(s_scl.h, s_scl.n)
        s_zz.inc(sc.dma_start(out=zz_d[r0:r0 + 128, 0:D], in_=ln1b_t[:]), 16)
        s_zz.inc(sc.dma_start(out=zz_d[r0:r0 + 128, D:2 * D], in_=ln1b_t[:]), 16)
    ga.close()

    def emit_a4_stage2_rt(gi, rtl, tiles):
        acc_t, rolls, x1_t, ln3b_t, w8s, idxs, W8_DONE = tiles
        rt = gi * NRTG + rtl
        r0 = rt * 128
        s_dve.inc(v.memset(acc_t[:], 0.0))
        for k in range(8):
            gslot = st2["gat_n"] % 2
            if st2["gat_n"] >= 2:
                gp.wait_ge(s_dve.h, st2["gat_free"][gslot])
            s_gat[gslot].inc(gp.indirect_dma_start(
                out=rolls[gslot][:], out_offset=None,
                in_=zz_d[0:R, :],
                in_offset=IndirectOffsetOnAxis(idxs[:, rtl * 8 + k:rtl * 8 + k + 1], 1)), 16)
            v.wait_ge(s_gat[gslot].h, s_gat[gslot].n)
            v.wait_ge(s_scl.h, W8_DONE)
            v.wait_ge(s_dve.h, s_dve.n)
            ins = v.scalar_tensor_tensor(acc_t[:], rolls[gslot][:],
                                         w8s[:, rtl * 8 + k:rtl * 8 + k + 1],
                                         acc_t[:], OP.mult, OP.add)
            s_dve.inc(ins)
            st2["gat_free"][gslot] = s_dve.n
            st2["gat_n"] += 1
        ln_stats(acc_t)
        gp.wait_ge(st2["x0_evt"][0], st2["x0_evt"][1])
        s_x0g.inc(gp.dma_start(out=x0_t[:], in_=x_in[r0:r0 + 128, :]), 16)
        sc.wait_ge(s_dve.h, s_dve.n)
        if st2["x1w"] > 0:
            sc.wait_ge(s_x1w[0].h, s_x1w[0].n)
        s_scl.inc(sc.activation(x1_t[:], acc_t[:], AF.Identity,
                                bias=nmrs_t[:], scale=rs_t[:]))
        v.wait_ge(s_scl.h, s_scl.n)
        v.wait_ge(s_x0g.h, s_x0g.n)
        ins = v.tensor_tensor(x1_t[:], x1_t[:], x0_t[:], OP.add)
        s_dve.inc(ins)
        st2["x0_evt"] = (s_dve.h, s_dve.n)
        ln_stats(x1_t)
        sc.wait_ge(s_dve.h, s_dve.n)
        if st2["l3w"] > 0:
            sc.wait_ge(s_l3w.h, s_l3w.n)
        s_scl.inc(sc.activation(ln3b_t[:], x1_t[:], AF.Identity,
                                bias=nmrs_t[:], scale=rs_t[:]))
        sc.wait_ge(s_scl.h, s_scl.n)
        s_x1w[0].inc(sc.dma_start(out=x1_d[r0:r0 + 128, :], in_=x1_t[:]), 16)
        st2["x1w"] += 1
        s_l3w.inc(sc.dma_start(out=ln3_d[r0:r0 + 128, :], in_=ln3b_t[:]), 16)
        st2["l3w"] += 1
        emit_dequant(2)

    st2 = {"gat_n": 0, "gat_free": [0, 0], "x0_evt": x0_free_evt,
           "x1w": 0, "l3w": 0}
    prev = None
    last_group_evac = 0

    for g in range(NG):
        sy.wait_ge(s_zz.h, s_zz.n)
        if g > 0:
            sy.wait_ge(s_pe.h, fpair_done[g * 16 - 1])
        for tc in range(32):
            s_l1t.inc(sy.dma_start(
                out=ln1T[:, tc * GRP:(tc + 1) * GRP],
                in_=zz_d[g * GRP:(g + 1) * GRP, tc * 128:(tc + 1) * 128],
                transpose=True), 16)

        ga2 = ExitStack()
        fcb = [ga2.enter_context(nc.sbuf_tensor(f"fcb{g}_{i}", [128, 4096], BF16)) for i in range(3)]
        fsb = [ga2.enter_context(nc.sbuf_tensor(f"fsb{g}_{i}", [128, 4096], BF16)) for i in range(3)]
        sq1 = ga2.enter_context(nc.sbuf_tensor(f"sq1_{g}", [128, 512], F32))

        def emit_fload(fb2):
            if not (0 <= fb2 < 16):
                return
            slot2 = fb2 % 3
            pair2 = g * 16 + fb2
            if fb2 >= 3:
                sy.wait_ge(s_pe.h, fpair_done[pair2 - 3])
            elif g > 0:
                sy.wait_ge(s_pe.h, fpair_done[g * 16 - 1])
                sy.wait_ge(s_dve.h, last_group_evac)
            s_fld[slot2].inc(sy.dma_start(out=fcb[slot2][:], in_=fc_d[fb2]), 16)
            s_fld[slot2].inc(sy.dma_start(out=fsb[slot2][:], in_=fs_d[fb2]), 16)
            fld_val[fb2] = (s_fld[slot2].h, s_fld[slot2].n)
        emit_fload(0)
        emit_fload(1)
        for fb in range(16):
            slot = fb % 3
            pair = g * 16 + fb
            emit_fload(fb + 2)
            bz_c, bz_s = 2 * (fb % 4), 2 * (fb % 4) + 1
            fh, fv = fld_val[fb]
            te.wait_ge(fh, fv)
            te.wait_ge(s_l1t.h, s_l1t.n)
            if pair >= 4:
                te.wait_ge(s_dve.h, evac_done[fb % 4])
            if g > 0 and fb < 8:
                te.wait_ge(s_dve.h, cread_done[(7, NRTG - 1)])
            for tc in range(32):
                te.matmul(ps[bz_c][:, 0:GRP], fcb[slot][:, tc * 128:(tc + 1) * 128],
                          ln1T[:, tc * GRP:(tc + 1) * GRP],
                          start=(tc == 0), stop=(tc == 31))
                mm = te.matmul(ps[bz_s][:, 0:GRP], fsb[slot][:, tc * 128:(tc + 1) * 128],
                               ln1T[:, tc * GRP:(tc + 1) * GRP],
                               start=(tc == 0), stop=(tc == 31))
                if tc == 31:
                    s_pe.inc(mm)
            fpair_done[pair] = s_pe.n
            sc.wait_ge(s_pe.h, s_pe.n)
            sc.wait_ge(s_dve.h, last_evac)
            s_scl.inc(sc.activation(sq1[:, 0:GRP], ps[bz_c][:, 0:GRP], AF.Square))
            s_scl.inc(sc.activation(PT[:, fb * GRP:(fb + 1) * GRP],
                                    ps[bz_s][:, 0:GRP], AF.Square))
            v.wait_ge(s_scl.h, s_scl.n)
            v.wait_ge(s_dve.h, s_dve.n)
            s_dve.inc(v.tensor_tensor(PT[:, fb * GRP:(fb + 1) * GRP],
                                      PT[:, fb * GRP:(fb + 1) * GRP],
                                      sq1[:, 0:GRP], OP.add))
            evac_done[fb % 4] = s_dve.n
            last_evac = s_dve.n
            if fb == 15:
                pt_done = s_dve.n
            emit_dequant(8)
            if prev is not None and fb % 4 == 3:
                emit_a4_stage2_rt(g - 1, fb // 4, prev[1])
        ga2.close()
        last_group_evac = last_evac
        if prev is not None:
            prev[0].close()
            prev = None

        ga = ExitStack()
        acc_t = ga.enter_context(nc.sbuf_tensor(f"acc{g}", [128, D], F32))
        rolls = [ga.enter_context(nc.sbuf_tensor(f"roll{g}_{i}", [128, D], BF16)) for i in range(2)]
        x1_t = ga.enter_context(nc.sbuf_tensor(f"x1t{g}", [128, D], F32))
        ln3b_t = ga.enter_context(nc.sbuf_tensor(f"ln3b{g}", [128, D], BF16))
        w8s = ga.enter_context(nc.sbuf_tensor(f"w8s{g}", [128, 8 * NRTG], F32))
        idxs = ga.enter_context(nc.sbuf_tensor(f"idxs{g}", [128, 8 * NRTG], I32))

        gccs = ExitStack()
        ccs = [gccs.enter_context(nc.sbuf_tensor(f"ccs{g}_{i}", [128, 512], BF16)) for i in range(2)]
        for fs_ in s_fld:
            sy.wait_ge(fs_.h, fs_.n)
        sy.wait_ge(s_pe.h, fpair_done[g * 16 + 15])
        sy.wait_ge(s_dve.h, s_dve.n)
        sy.wait_ge(s_scl.h, s_scl.n)
        sy.wait_ge(s_x1w[0].h, s_x1w[0].n)
        sy.wait_ge(s_l3w.h, s_l3w.n)
        for lc in range(8):
            for fb in range(16):
                cn = g * 128 + lc * 16 + fb
                cslot = cn % 2
                if cn >= 2:
                    sy.wait_ge(s_pe.h, cuse_done[cn - 2])
                s_cld[cslot].inc(sy.dma_start(
                    out=ccs[cslot][:],
                    in_=ccos_d[fb * 128:(fb + 1) * 128, lc * 512:(lc + 1) * 512]), 16)
                te.wait_ge(s_cld[cslot].h, s_cld[cslot].n)
                if fb == 0:
                    te.wait_ge(s_dve.h, pt_done)
                    if lc >= 1:
                        te.wait_ge(s_dve.h, cread_done[(lc - 1, NRTG - 1)])
                for rtl in range(NRTG):
                    last = te.matmul(
                        ps[rtl][:, 0:512],
                        PT[:, fb * GRP + rtl * 128:fb * GRP + (rtl + 1) * 128],
                        ccs[cslot][:], start=(fb == 0), stop=(fb == 15))
                    if fb == 15 or rtl == NRTG - 1:
                        s_pe.inc(last)
                    if fb == 15:
                        cstop_done[(lc, rtl)] = s_pe.n
                cuse_done[cn] = s_pe.n
            for rtl in range(NRTG):
                v.wait_ge(s_pe.h, cstop_done[(lc, rtl)])
                o8 = stkv[:, rtl * 64 + lc * 8: rtl * 64 + lc * 8 + 8]
                s_dve.inc(v.max(out=o8, in_=ps[rtl][:, 0:512]))
                v.wait_ge(s_dve.h, s_dve.n)
                s_dve.inc(v.max_index(out=tmpi8[:], in_max=o8,
                                      in_values=ps[rtl][:, 0:512]))
                v.wait_ge(s_dve.h, s_dve.n)
                ins = v.tensor_scalar(stki[:, rtl * 64 + lc * 8: rtl * 64 + lc * 8 + 8],
                                      tmpi8[:], float(512 * lc), None, OP.add)
                s_dve.inc(ins)
                cread_done[(lc, rtl)] = s_dve.n
            emit_dequant(4)
        gccs.close()

        for rtl in range(NRTG):
            rt = g * NRTG + rtl
            r0 = rt * 128
            sv = stkv[:, rtl * 64:(rtl + 1) * 64]
            si = stki[:, rtl * 64:(rtl + 1) * 64]
            v.wait_ge(s_dve.h, s_dve.n)
            v.wait_ge(s_scl.h, s_scl.n)
            s_dve.inc(v.max(out=g8[:], in_=sv))
            v.wait_ge(s_dve.h, s_dve.n)
            s_dve.inc(v.tensor_scalar_mul(negmax[:], g8[:, 0:1], -1.0))
            sc.wait_ge(s_dve.h, s_dve.n)
            sc.wait_ge(s_scl.h, s_scl.n)
            s_scl.inc(sc.activation(exp8[:], g8[:], AF.Exp, bias=negmax[:], scale=1.0))
            v.wait_ge(s_scl.h, s_scl.n)
            s_dve.inc(v.reduce_sum(sume[:], exp8[:], axis=AX.X))
            v.wait_ge(s_dve.h, s_dve.n)
            s_dve.inc(v.reciprocal(rsum[:], sume[:]))
            sc.wait_ge(s_dve.h, s_dve.n)
            s_scl.inc(sc.activation(w8s[:, rtl * 8:rtl * 8 + 8], exp8[:], AF.Identity,
                                    bias=0.0, scale=rsum[:]))
            for j in range(8):
                v.wait_ge(s_dve.h, s_dve.n)
                s_dve.inc(v.tensor_scalar(mask64[:], sv, g8[:, j:j + 1], None,
                                          OP.is_equal))
                v.wait_ge(s_dve.h, s_dve.n)
                s_dve.inc(v.tensor_mul(mask64[:], mask64[:], si))
                v.wait_ge(s_dve.h, s_dve.n)
                s_dve.inc(v.reduce_max(d8f[:, j:j + 1], mask64[:], axis=AX.X))
            gp.wait_ge(s_dve.h, rowb_used[0])
            s_gps.inc(gp.iota(rowb_i[:], pattern=[[0, 1]], base=r0 * 2 * D,
                              channel_multiplier=2 * D))
            v.wait_ge(s_gps.h, s_gps.n)
            s_dve.inc(v.tensor_copy(rowb_f[:], rowb_i[:]))
            rowb_used[0] = s_dve.n
            v.wait_ge(s_dve.h, s_dve.n)
            s_dve.inc(v.tensor_scalar(d8f[:], d8f[:], rowb_f[:, 0:1], None, OP.add))
            v.wait_ge(s_dve.h, s_dve.n)
            s_dve.inc(v.tensor_copy(idxs[:, rtl * 8:rtl * 8 + 8], d8f[:]))
        W8_DONE = s_scl.n
        gp.wait_ge(s_dve.h, s_dve.n)
        prev = (ga, (acc_t, rolls, x1_t, ln3b_t, w8s, idxs, W8_DONE))

    for rtl in range(NRTG):
        emit_a4_stage2_rt(NG - 1, rtl, prev[1])
    prev[0].close()
    x0_free_evt = st2["x0_evt"]

    aph.close()

    bph = ExitStack()
    BE = bph.enter_context
    zs_g = BE(nc.sbuf_tensor([GK, IDIM], BF16))
    zs_u = BE(nc.sbuf_tensor([GK, IDIM], BF16))
    g32_sb = BE(nc.sbuf_tensor([128, GK * GK], BF16))
    sln3 = BE(nc.sbuf_tensor([GK, R], BF16))

    zb = ExitStack()
    znib = zb.enter_context(nc.sbuf_tensor([GI, IDIM], mybir.dt.int16))
    zq_gu = zb.enter_context(nc.sbuf_tensor([64, GUW], I32))
    zq_dn = zb.enter_context(nc.sbuf_tensor([GI, DW], I32))
    sc_gu = zb.enter_context(nc.sbuf_tensor("scgu_b1", [64, IDIM], BF16))
    sc_dn = zb.enter_context(nc.sbuf_tensor("scdn_b1", [GI, D], BF16))
    sy.wait_ge(s_x1w[0].h, s_x1w[0].n)
    sy.wait_ge(s_l3w.h, s_l3w.n)
    for dst, src in ((zq_gu[0:GK, :], gz), (zq_gu[GK:64, :], uz), (zq_dn[:], dz),
                     (sc_gu[:], scb_gu_d), (sc_dn[:], scb_dn_d)):
        s_set.inc(sy.dma_start(out=dst, in_=src[:]), 16)
    v.wait_ge(s_set.h, s_set.n)
    v.wait_ge(s_x1w[0].h, s_x1w[0].n)
    v.wait_ge(s_l3w.h, s_l3w.n)
    zq16 = zq_gu[:].bitcast(mybir.dt.int16)
    for j in range(8):
        pos = POS_OF_SHIFT[j]
        s_dve.inc(v.tensor_scalar(znib[0:64, pos::8], zq16[:, j // 4::2],
                                  4 * (j % 4), 15,
                                  OP.logical_shift_right, OP.bitwise_and))
    v.wait_ge(s_dve.h, s_dve.n)
    s_dve.inc(v.scalar_tensor_tensor(zs_g[:], znib[0:GK, :], -1.0,
                                     sc_gu[0:GK, :], OP.mult, OP.mult))
    s_dve.inc(v.scalar_tensor_tensor(zs_u[:], znib[GK:64, :], -1.0,
                                     sc_gu[GK:64, :], OP.mult, OP.mult))
    v.wait_ge(s_dve.h, s_dve.n)
    zq16d = zq_dn[:].bitcast(mybir.dt.int16)
    for j in range(8):
        pos = POS_OF_SHIFT[j]
        s_dve.inc(v.tensor_scalar(znib[0:GI, pos:pos + 8 * (DW - 1) + 1:8],
                                  zq16d[:, j // 4::2], 4 * (j % 4), 15,
                                  OP.logical_shift_right, OP.bitwise_and))
    v.wait_ge(s_dve.h, s_dve.n)
    s_dve.inc(v.scalar_tensor_tensor(zs_dn[:], znib[0:GI, 0:D], -1.0,
                                     sc_dn[:], OP.mult, OP.mult))
    ZTAB_DONE = s_dve.n
    zb.close()
    ln3T = BE(nc.sbuf_tensor([128, 32 * R], BF16))
    wgub = [BE(nc.sbuf_tensor(f"wgub{i}", [128, 8 * 256], BF16)) for i in range(3)]
    silu_t = [BE(nc.sbuf_tensor(f"silu{i}", [128, 512], BF16)) for i in range(2)]
    h_t = [BE(nc.sbuf_tensor(f"ht{i}", [128, 512], BF16)) for i in range(2)]

    sy.wait_ge(s_l3w.h, s_l3w.n)
    sy.wait_ge(s_x1w[0].h, s_x1w[0].n)
    sy.wait_ge(s_dve.h, ZTAB_DONE)
    s_set.inc(sy.dma_start(out=g32_sb[:], in_=g32_d[:]), 16)
    for tc in range(32):
        s_l3t.inc(sy.dma_start(out=ln3T[:, tc * R:(tc + 1) * R],
                               in_=ln3_d[0:R, tc * 128:(tc + 1) * 128],
                               transpose=True), 16)
    te.wait_ge(s_l3t.h, s_l3t.n)
    te.wait_ge(s_dve.h, cread_done[(7, NRTG - 1)])
    te.wait_ge(s_set.h, s_set.n)
    for t in range(NTC):
        for kc in range(GK):
            mm = te.matmul(ps[6 + t][0:GK, 0:TCS],
                           g32_sb[:, kc * GK:(kc + 1) * GK],
                           ln3T[:, kc * R + t * TCS: kc * R + (t + 1) * TCS],
                           start=(kc == 0), stop=(kc == GK - 1))
            if kc == GK - 1:
                s_pe.inc(mm)
        v.wait_ge(s_pe.h, s_pe.n)
        s_dve.inc(v.tensor_copy(sln3[:, t * TCS:(t + 1) * TCS], ps[6 + t][0:GK, 0:TCS]))
    SLN3_DONE = s_dve.n

    emit_dequant(N_GU - dqs["i"] if dqs["i"] < N_GU else 0)
    for ws, wv in zip(s_wwr, dqs["gu_wwr_totals"]):
        te.wait_ge(ws.h, wv)
    te.wait_ge(s_dve.h, SLN3_DONE)
    hw_n = 0
    wq_n = 0
    wgu_done = {}
    bank_free = {}
    silu_used = [0, 0]
    ffnfree_scl = 0
    ffnfree_dve = 0
    for icb in range(43):
        for q in range(8):
            wslot = wq_n % 3
            if wq_n >= 3:
                sy.wait_ge(s_pe.h, wgu_done[wq_n - 3])
            if wq_n == 0:
                sy.wait_ge(s_l3w.h, s_l3w.n)
                for ws, wv in zip(s_wwr, dqs["gu_wwr_totals"]):
                    sy.wait_ge(ws.h, wv)
            w_i = q % 2
            kq = q // 2
            src = (wg_d if w_i == 0 else wu_d)[icb, kq * 8:(kq + 1) * 8]
            s_wgu[wslot].inc(sy.dma_start(
                out=wgub[wslot][:, 0:8 * 256].rearrange("p (k c) -> p k c", c=256),
                in_=src.rearrange("k p c -> p k c")), 16)
            wq_n += 1
            te.wait_ge(s_wgu[wslot].h, s_wgu[wslot].n)
            for kk in range(8):
                kc = kq * 8 + kk
                for ic2 in range(2):
                    lhs = wgub[wslot][:, kk * 256 + ic2 * 128: kk * 256 + (ic2 + 1) * 128]
                    for t in range(NTC):
                        bank = 4 * w_i + 2 * ic2 + t
                        if kc == 0 and icb >= 1:
                            fs, fv = bank_free[bank]
                            te.wait_ge(fs, fv)
                        last = te.matmul(ps[bank][:, 0:TCS], lhs,
                                  ln3T[:, kc * R + t * TCS: kc * R + (t + 1) * TCS],
                                  start=(kc == 0), stop=False)
            s_pe.inc(last)
            wgu_done[wq_n - 1] = s_pe.n
        for ic2 in range(2):
            ic = icb * 2 + ic2
            for t in range(NTC):
                bg = 2 * ic2 + t
                bu = 4 + 2 * ic2 + t
                mm = te.matmul(ps[bg][:, 0:TCS],
                               zs_g[:, ic * 128:(ic + 1) * 128],
                               sln3[:, t * TCS:(t + 1) * TCS],
                               start=False, stop=True)
                s_pe.inc(mm)
                zg = s_pe.n
                mm = te.matmul(ps[bu][:, 0:TCS],
                               zs_u[:, ic * 128:(ic + 1) * 128],
                               sln3[:, t * TCS:(t + 1) * TCS],
                               start=False, stop=True)
                s_pe.inc(mm)
                zu = s_pe.n
                ssl = hw_n % 2
                sc.wait_ge(s_pe.h, zg)
                if hw_n >= 2:
                    sc.wait_ge(s_dve.h, silu_used[ssl])
                ins = sc.activation(silu_t[ssl][:, 0:TCS], ps[bg][:, 0:TCS], AF.Silu)
                s_scl.inc(ins)
                bank_free[bg] = (s_scl.h, s_scl.n)
                hslot = hw_n % 2
                if hw_n >= 2:
                    v.wait_ge(s_hw[hslot].h, 16 * (hw_n // 2))
                v.wait_ge(s_scl.h, s_scl.n)
                v.wait_ge(s_pe.h, zu)
                ins = v.tensor_tensor(h_t[hslot][:, 0:TCS], silu_t[ssl][:, 0:TCS],
                                      ps[bu][:, 0:TCS], OP.mult)
                s_dve.inc(ins)
                silu_used[ssl] = s_dve.n
                bank_free[bu] = (s_dve.h, s_dve.n)
                gp.wait_ge(s_dve.h, s_dve.n)
                s_hw[hslot].inc(gp.dma_start(
                    out=h_d[ic, :, t * TCS:(t + 1) * TCS],
                    in_=h_t[hslot][:, 0:TCS]), 16)
                hw_n += 1
        ffnfree_scl = s_scl.n
        ffnfree_dve = s_dve.n
        emit_dequant(4)
    emit_dequant(N_ALL)
    bph.close()
    s1.close()

    dph = ExitStack()
    DE = dph.enter_context
    hg_sb = DE(nc.sbuf_tensor([GI, R], BF16))
    h_sb = DE(nc.sbuf_tensor([128, GI * R], BF16))
    gph = ExitStack()
    g86_sb = gph.enter_context(nc.sbuf_tensor([128, GI * GI], BF16))

    sy.wait_ge(s_scl.h, ffnfree_scl)
    sy.wait_ge(s_dve.h, ffnfree_dve)
    sy.wait_ge(s_hw[0].h, s_hw[0].n)
    sy.wait_ge(s_hw[1].h, s_hw[1].n)
    for ws in s_wwr + s_wwrg:
        sy.wait_ge(ws.h, ws.n)
    s_set.inc(sy.dma_start(out=g86_sb[:], in_=g86_d[:]), 16)
    for ic in range(GI):
        s_hld.inc(sy.dma_start(out=h_sb[:, ic * R:(ic + 1) * R], in_=h_d[ic]), 16)
    te.wait_ge(s_hld.h, s_hld.n)
    te.wait_ge(s_set.h, s_set.n)
    for t in range(NTC):
        for ic in range(GI):
            mm = te.matmul(ps[t][0:GI, 0:TCS], g86_sb[:, ic * GI:(ic + 1) * GI],
                           h_sb[:, ic * R + t * TCS: ic * R + (t + 1) * TCS],
                           start=(ic == 0), stop=(ic == GI - 1))
            if ic == GI - 1:
                s_pe.inc(mm)
        v.wait_ge(s_pe.h, s_pe.n)
        s_dve.inc(v.tensor_copy(hg_sb[:, t * TCS:(t + 1) * TCS], ps[t][0:GI, 0:TCS]))
    HG_DONE = s_dve.n
    gph.close()

    wd_sb = [DE(nc.sbuf_tensor(f"wdsb{i}", [128, 8 * 256], BF16)) for i in range(3)]
    fT_t = [DE(nc.sbuf_tensor(f"fTt{i}", [128, 512], BF16)) for i in range(2)]
    SLABS = []
    for dchb in range(16):
        i0 = 0
        while i0 < GI:
            nic = min(8, GI - i0)
            SLABS.append((dchb, i0, nic))
            i0 += nic
    fT_n = 0
    wd_done = {}
    dfree_done = {}
    for si, (dchb, i0, nic) in enumerate(SLABS):
        wsl = si % 3
        if si >= 3:
            sy.wait_ge(s_pe.h, wd_done[si - 3])
        if si == 0:
            sy.wait_ge(s_dve.h, HG_DONE)
        s_wdl[wsl].inc(sy.dma_start(
            out=wd_sb[wsl][:, 0:nic * 256].rearrange("p (i c) -> p i c", c=256),
            in_=wd_d[dchb, i0:i0 + nic].rearrange("i p c -> p i c")), 16)
        te.wait_ge(s_wdl[wsl].h, s_wdl[wsl].n)
        if i0 == 0:
            if dchb == 0:
                te.wait_ge(s_dve.h, HG_DONE)
            if dchb >= 2:
                te.wait_ge(s_scl.h, dfree_done[dchb - 2])
        last = None
        for ii in range(nic):
            ic = i0 + ii
            for dch2 in range(2):
                lhs = wd_sb[wsl][:, ii * 256 + dch2 * 128: ii * 256 + (dch2 + 1) * 128]
                for t in range(NTC):
                    bank = 4 * (dchb % 2) + 2 * dch2 + t
                    last = te.matmul(ps[bank][:, 0:TCS], lhs,
                                     h_sb[:, ic * R + t * TCS: ic * R + (t + 1) * TCS],
                                     start=(ic == 0), stop=False)
        s_pe.inc(last)
        wd_done[si] = s_pe.n
        if i0 + nic == GI:
            for dch2 in range(2):
                dch = 2 * dchb + dch2
                for t in range(NTC):
                    bank = 4 * (dchb % 2) + 2 * dch2 + t
                    mm = te.matmul(ps[bank][:, 0:TCS],
                                   zs_dn[:, dch * 128:(dch + 1) * 128],
                                   hg_sb[:, t * TCS:(t + 1) * TCS],
                                   start=False, stop=True)
                    s_pe.inc(mm)
            pe_done = s_pe.n
            for dch2 in range(2):
                dch = 2 * dchb + dch2
                for t in range(NTC):
                    bank = 4 * (dchb % 2) + 2 * dch2 + t
                    fsl = fT_n % 2
                    if fT_n >= 2:
                        sc.wait_ge(s_fT[fsl].h, 16 * (fT_n // 2))
                    sc.wait_ge(s_pe.h, pe_done)
                    ins = sc.activation(fT_t[fsl][:, 0:TCS], ps[bank][:, 0:TCS],
                                        AF.Identity, bias=0.0, scale=1.0)
                    s_scl.inc(ins)
                    sc.wait_ge(s_scl.h, s_scl.n)
                    s_fT[fsl].inc(sc.dma_start(
                        out=ffnT_d[dch * 128:(dch + 1) * 128, t * TCS:(t + 1) * TCS],
                        in_=fT_t[fsl][:, 0:TCS]), 16)
                    fT_n += 1
            dfree_done[dchb] = s_scl.n
    dph.close()

    pph = ExitStack()
    PE_ = pph.enter_context
    ftok = [PE_(nc.sbuf_tensor(f"ftok{i}", [128, D], BF16)) for i in range(2)]
    x1l_t = [PE_(nc.sbuf_tensor(f"x1l{i}", [128, D], F32)) for i in range(2)]
    ln4n_t = PE_(nc.sbuf_tensor([128, D], F32))
    out_t = [PE_(nc.sbuf_tensor(f"outt{i}", [128, D], F32)) for i in range(2)]

    sy.wait_ge(s_fT[0].h, s_fT[0].n)
    sy.wait_ge(s_fT[1].h, s_fT[1].n)
    sy.wait_ge(s_x1w[0].h, s_x1w[0].n)
    p4_free = {}
    for rt in range(NRT):
        r0 = rt * 128
        sl = rt % 2
        if rt >= 2:
            sy.wait_ge(s_dve.h, p4_free[rt - 2])
        s_fTl[sl].inc(sy.dma_start(out=ftok[sl][:], in_=ffnT_d[0:D, r0:r0 + 128],
                                   transpose=True), 16)
        s_x1l[sl].inc(sy.dma_start(out=x1l_t[sl][:], in_=x1_d[r0:r0 + 128, :]), 16)
        v.wait_ge(s_fTl[sl].h, s_fTl[sl].n)
        ln_stats(ftok[sl])
        sc.wait_ge(s_dve.h, s_dve.n)
        s_scl.inc(sc.activation(ln4n_t[:], ftok[sl][:], AF.Identity,
                                bias=nmrs_t[:], scale=rs_t[:]))
        v.wait_ge(s_scl.h, s_scl.n)
        v.wait_ge(s_x1l[sl].h, s_x1l[sl].n)
        if rt >= 2:
            v.wait_ge(s_out[sl].h, 16 * (rt // 2))
        ins = v.tensor_add(out_t[sl][:], ln4n_t[:], x1l_t[sl][:])
        s_dve.inc(ins)
        p4_free[rt] = s_dve.n
        sy.wait_ge(s_dve.h, s_dve.n)
        s_out[sl].inc(sy.dma_start(out=out_d[r0:r0 + 128, :], in_=out_t[sl][:]), 16)
    sy.wait_ge(s_out[0].h, s_out[0].n)
    sy.wait_ge(s_out[1].h, s_out[1].n)

    pph.close()
    ctx.close()
    return nc


_PROG_CACHE = {}


def _get_prog(R):
    if R not in _PROG_CACHE:
        _PROG_CACHE[R] = build_program(R)
    return _PROG_CACHE[R]


def kernel(L0_attn_x, L0_ffn_gate_qweight, L0_ffn_gate_scales, L0_ffn_gate_qzeros,
           L0_ffn_up_qweight, L0_ffn_up_scales, L0_ffn_up_qzeros,
           L0_ffn_down_qweight, L0_ffn_down_scales, L0_ffn_down_qzeros):
    x = np.ascontiguousarray(np.asarray(L0_attn_x, dtype=np.float32)).reshape(8192, D)
    R = 8192 // 8
    cst = _consts()
    nc = _get_prog(R)
    shared = dict(
        gq=np.ascontiguousarray(np.asarray(L0_ffn_gate_qweight, dtype=np.int32)),
        gs=np.ascontiguousarray(np.asarray(L0_ffn_gate_scales, dtype=np.float32)),
        gz=np.ascontiguousarray(np.asarray(L0_ffn_gate_qzeros, dtype=np.int32)),
        uq=np.ascontiguousarray(np.asarray(L0_ffn_up_qweight, dtype=np.int32)),
        us=np.ascontiguousarray(np.asarray(L0_ffn_up_scales, dtype=np.float32)),
        uz=np.ascontiguousarray(np.asarray(L0_ffn_up_qzeros, dtype=np.int32)),
        dq=np.ascontiguousarray(np.asarray(L0_ffn_down_qweight, dtype=np.int32)),
        ds=np.ascontiguousarray(np.asarray(L0_ffn_down_scales, dtype=np.float32)),
        dz=np.ascontiguousarray(np.asarray(L0_ffn_down_qzeros, dtype=np.int32)),
        fc4=cst["fc4"], fs4=cst["fs4"], ccos=cst["ccos"],
        g32=cst["g32"], g86=cst["g86"])
    in_maps = [dict(shared, x=np.ascontiguousarray(x[c * R:(c + 1) * R]))
               for c in range(8)]
    res = run_bass_kernel_spmd(nc, in_maps, core_ids=list(range(8)))
    out = np.concatenate([np.asarray(r["out"], dtype=np.float32)
                          for r in res.results], axis=0)
    return out.reshape(4, 2048, D)


# version 14
# speedup vs baseline: 1.0021x; 1.0021x over previous
import math
import numpy as np
import ml_dtypes
import concourse.bass as bass
import concourse.mybir as mybir
from concourse.bass import IndirectOffsetOnAxis
from concourse.bass_utils import run_bass_kernel_spmd
from contextlib import ExitStack

F32 = mybir.dt.float32
BF16 = mybir.dt.bfloat16
I32 = mybir.dt.int32
UI32 = mybir.dt.uint32
AF = mybir.ActivationFunctionType
OP = mybir.AluOpType
AX = mybir.AxisListType

D = 4096
IDIM = 11008
NF = 2048
GK = 32
GI = 86
EPS = 1e-5
GUW = IDIM // 8
DW = D // 8
GU_SLICES = [(0, 2816), (2816, 2816), (5632, 2816), (8448, 2560)]
DN_SLICES = [(0, 2048), (2048, 2048)]
POS_OF_SHIFT = [0, 2, 4, 6, 1, 3, 5, 7]


def _bf(x):
    return np.asarray(x, dtype=ml_dtypes.bfloat16)


_CONST = {}


def _consts():
    if not _CONST:
        t = np.arange(D, dtype=np.int64)
        f = np.arange(1, NF + 1, dtype=np.int64)
        ph = ((t[:, None] * f[None, :]) % D).astype(np.float64) * (2 * math.pi / D)
        fc4 = _bf(np.cos(ph)).reshape(32, 128, 16, 128).transpose(2, 1, 0, 3).copy()
        fs4 = _bf(np.sin(ph)).reshape(32, 128, 16, 128).transpose(2, 1, 0, 3).copy()
        ll = np.arange(D, dtype=np.int64)
        phc = ((f[:, None] * ll[None, :]) % D).astype(np.float64) * (2 * math.pi / D)
        cc = np.cos(phc) * (2.0 / D)
        cc[NF - 1, :] *= 0.5
        g32 = (np.arange(D)[:, None] // 128 == np.arange(GK)[None, :]).astype(np.float32)
        g86 = (np.arange(IDIM)[:, None] // 128 == np.arange(GI)[None, :]).astype(np.float32)
        _CONST.update(
            fc4=fc4, fs4=fs4, ccos=_bf(cc).copy(),
            g32=_bf(g32).reshape(GK, 128, GK).transpose(1, 0, 2).copy(),
            g86=_bf(g86).reshape(GI, 128, GI).transpose(1, 0, 2).copy())
    return _CONST


class Sem:
    __slots__ = ("h", "n")

    def __init__(self, ctx, nc, name):
        self.h = ctx.enter_context(nc.semaphore(name))
        self.n = 0

    def inc(self, instr, k=1):
        instr.then_inc(self.h, k)
        self.n += k
        return self.n


def build_program(R):
    NRT = R // 128
    GRP = min(512, R)
    NG = R // GRP
    NRTG = GRP // 128
    TCS = min(512, R)
    NTC = R // TCS

    nc = bass.Bass("TRN2", target_bir_lowering=False)

    x_in = nc.dram_tensor("x", [R, D], F32, kind="ExternalInput").ap()
    out_d = nc.dram_tensor("out", [R, D], F32, kind="ExternalOutput").ap()
    gq = nc.dram_tensor("gq", [D, GUW], I32, kind="ExternalInput").ap()
    gs = nc.dram_tensor("gs", [GK, IDIM], F32, kind="ExternalInput").ap()
    gz = nc.dram_tensor("gz", [GK, GUW], I32, kind="ExternalInput").ap()
    uq = nc.dram_tensor("uq", [D, GUW], I32, kind="ExternalInput").ap()
    us = nc.dram_tensor("us", [GK, IDIM], F32, kind="ExternalInput").ap()
    uz = nc.dram_tensor("uz", [GK, GUW], I32, kind="ExternalInput").ap()
    dq = nc.dram_tensor("dq", [IDIM, DW], I32, kind="ExternalInput").ap()
    ds = nc.dram_tensor("ds", [GI, D], F32, kind="ExternalInput").ap()
    dz = nc.dram_tensor("dz", [GI, DW], I32, kind="ExternalInput").ap()
    fc_d = nc.dram_tensor("fc4", [16, 128, 32, 128], BF16, kind="ExternalInput").ap()
    fs_d = nc.dram_tensor("fs4", [16, 128, 32, 128], BF16, kind="ExternalInput").ap()
    ccos_d = nc.dram_tensor("ccos", [NF, D], BF16, kind="ExternalInput").ap()
    g32_d = nc.dram_tensor("g32", [128, GK, GK], BF16, kind="ExternalInput").ap()
    g86_d = nc.dram_tensor("g86", [128, GI, GI], BF16, kind="ExternalInput").ap()

    zz_d = nc.dram_tensor("zz_s", [R, 2 * D], BF16, kind="Internal").ap()
    x1_d = nc.dram_tensor("x1_s", [R, D], F32, kind="Internal").ap()
    ln3_d = nc.dram_tensor("ln3_s", [R, D], BF16, kind="Internal").ap()
    wg_d = nc.dram_tensor("wg_s", [43, GK, 128, 256], BF16, kind="Internal").ap()
    wu_d = nc.dram_tensor("wu_s", [43, GK, 128, 256], BF16, kind="Internal").ap()
    wd_d = nc.dram_tensor("wd_s", [16, GI, 128, 256], BF16, kind="Internal").ap()
    h_d = nc.dram_tensor("h_s", [GI, 128, R], BF16, kind="Internal").ap()
    ffnT_d = nc.dram_tensor("ffnT_s", [D, R], BF16, kind="Internal").ap()
    scb_gu_d = nc.dram_tensor("scbgu_s", [64, IDIM], BF16, kind="Internal").ap()
    scb_dn_d = nc.dram_tensor("scbdn_s", [GI, D], BF16, kind="Internal").ap()

    ctx = ExitStack()
    E = ctx.enter_context
    SS = lambda name: Sem(ctx, nc, name)

    ps = [E(nc.psum_tensor(f"psb{i}", [128, 512], F32)) for i in range(8)]

    mv = E(nc.sbuf_tensor([128, 2], F32))
    bnst = E(nc.sbuf_tensor([128, 48], F32))
    std_t = E(nc.sbuf_tensor([128, 1], F32))
    rs_t = E(nc.sbuf_tensor([128, 1], F32))
    nmrs_t = E(nc.sbuf_tensor([128, 1], F32))
    stkv = E(nc.sbuf_tensor([128, 64 * 4], F32))
    stki = E(nc.sbuf_tensor([128, 64 * 4], F32))
    tmpi8 = E(nc.sbuf_tensor([128, 8], UI32))
    g8 = E(nc.sbuf_tensor([128, 8], F32))
    negmax = E(nc.sbuf_tensor([128, 1], F32))
    exp8 = E(nc.sbuf_tensor([128, 8], F32))
    sume = E(nc.sbuf_tensor([128, 1], F32))
    rsum = E(nc.sbuf_tensor([128, 1], F32))
    w8 = E(nc.sbuf_tensor([128, 8], F32))
    mask64 = E(nc.sbuf_tensor([128, 64], F32))
    cand64 = E(nc.sbuf_tensor([128, 64], F32))
    d8f = E(nc.sbuf_tensor([128, 8], F32))
    idxf = E(nc.sbuf_tensor([128, 8], F32))
    idx_i = E(nc.sbuf_tensor([128, 8], I32))
    rowb_i = E(nc.sbuf_tensor([128, 1], I32))
    rowb_f = E(nc.sbuf_tensor([128, 1], F32))
    zs_dn = E(nc.sbuf_tensor([GI, D], BF16))
    eps_t = E(nc.sbuf_tensor([128, 1], F32))
    hg_sb = E(nc.sbuf_tensor([GI, R], BF16))

    s_set = SS("set")
    s_dve = SS("dve")
    s_scl = SS("scl")
    s_gps = SS("gps")
    s_pe = SS("pe")
    s_x0 = SS("x0")
    s_x0f = SS("x0f")
    s_zz = SS("zz")
    s_l1t = SS("l1t")
    s_fld = [SS(f"fld{i}") for i in range(3)]
    s_ffree = SS("ffree")
    s_cld = [SS(f"cld{i}") for i in range(2)]
    s_cuse = SS("cuse")
    s_cstop = SS("cstop")
    s_cread = SS("cread")
    s_pk = [SS(f"pk{i}") for i in range(3)]
    s_pkg = [SS(f"pkg{i}") for i in range(3)]
    s_scrg = [SS(f"scrg{i}") for i in range(3)]
    s_wwrg = [SS(f"wwrg{i}") for i in range(3)]
    s_scr = [SS(f"scr{i}") for i in range(3)]
    s_scw = SS("scw")
    s_wwr = [SS(f"wwr{i}") for i in range(3)]
    s_gat = [SS("gat0"), SS("gat1")]
    s_gatf = SS("gatf")
    s_x1w = [SS("x1w0"), SS("x1w1")]
    s_l3w = SS("l3w")
    s_l3t = SS("l3t")
    s_wgu = [SS(f"wgu{i}") for i in range(3)]
    s_wguf = SS("wguf")
    s_hw = [SS("hw0"), SS("hw1")]
    s_ffnfree = SS("ffnfree")
    s_hld = SS("hld")
    s_wdl = [SS(f"wdl{i}") for i in range(3)]
    s_wdf = SS("wdf")
    s_dfree = SS("dfree")
    s_fT = [SS("fT0"), SS("fT1")]
    s_fTl = [SS("fTl0"), SS("fTl1")]
    s_x1l = [SS("x1l0"), SS("x1l1")]
    s_p4f = SS("p4f")
    s_out = [SS("out0"), SS("out1")]

    blk = E(nc.Block())
    te, v, sc, gp, sy = nc.tensor, nc.vector, nc.scalar, nc.gpsimd, nc.sync

    s1 = ExitStack()
    S1 = s1.enter_context
    sc_gu = S1(nc.sbuf_tensor([64, IDIM], BF16))
    sc_dn = S1(nc.sbuf_tensor([GI, D], BF16))
    pk_sb = [S1(nc.sbuf_tensor(f"pk{i}", [128, 352], I32)) for i in range(3)]
    nib_sb = [S1(nc.sbuf_tensor(f"nib{i}", [128, 2816], mybir.dt.int16)) for i in range(2)]
    sbc_sb = [S1(nc.sbuf_tensor(f"sbc{i}", [128, 2816], BF16)) for i in range(3)]

    s_dve.inc(v.memset(eps_t[:], EPS))
    p0 = ExitStack()
    sc32_gu = p0.enter_context(nc.sbuf_tensor([64, IDIM], F32))
    sc32_dn = p0.enter_context(nc.sbuf_tensor([GI, D], F32))

    for dst, src in ((sc32_gu[0:GK, :], gs), (sc32_gu[GK:64, :], us),
                     (sc32_dn[:], ds)):
        s_set.inc(sy.dma_start(out=dst, in_=src[:]), 16)
    v.wait_ge(s_set.h, s_set.n)
    s_dve.inc(v.tensor_copy(sc_gu[:], sc32_gu[:]))
    s_dve.inc(v.tensor_copy(sc_dn[:], sc32_dn[:]))
    P0_DONE = s_dve.n
    sc.wait_ge(s_dve.h, P0_DONE)
    s_scw.inc(sc.dma_start(out=scb_gu_d[:], in_=sc_gu[:]), 16)
    s_scw.inc(sc.dma_start(out=scb_dn_d[:], in_=sc_dn[:]), 16)
    p0.close()

    def ln_stats(src_ap, fd=D):
        nch = max(1, fd // 512)
        cw = fd // nch
        v.wait_ge(s_dve.h, s_dve.n)
        for c in range(nch):
            s_dve.inc(v.bn_stats(bnst[:, 6 * c:6 * c + 6],
                                 src_ap[:, c * cw:(c + 1) * cw]))
        v.wait_ge(s_dve.h, s_dve.n)
        s_dve.inc(v.bn_aggr(mv[:], bnst[:, 0:6 * nch]))
        sc.wait_ge(s_dve.h, s_dve.n)
        s_scl.inc(sc.activation(std_t[:], mv[:, 1:2], AF.Sqrt, bias=eps_t[:], scale=1.0))
        v.wait_ge(s_scl.h, s_scl.n)
        s_dve.inc(v.reciprocal(rs_t[:], std_t[:]))
        v.wait_ge(s_dve.h, s_dve.n)
        s_dve.inc(v.scalar_tensor_tensor(nmrs_t[:], mv[:, 0:1], -1.0, rs_t[:],
                                         OP.mult, OP.mult))
        return s_dve.n

    DQ_UNITS = ([("g", kc, sl) for kc in range(GK) for sl in range(4)]
                + [("u", kc, sl) for kc in range(GK) for sl in range(4)]
                + [("d", ic, sl) for ic in range(GI) for sl in range(2)])
    N_GU = 256
    N_ALL = len(DQ_UNITS)
    dqs = {"u": 0, "i": 0, "mdone": [0, 0], "pkslot": 0,
           "pkf": [0, 0, 0], "sbcslot": 0}

    def dequant_unit(which, kc, sl):
        u = dqs["u"]
        dqs["u"] += 1
        slot = u % 2
        col0, ncol = (GU_SLICES[sl] if which != "d" else DN_SLICES[sl])
        nw = ncol // 8
        if which == "g":
            src_q, wdst, scsrc = gq, wg_d, scb_gu_d[kc:kc + 1, :]
        elif which == "u":
            src_q, wdst, scsrc = uq, wu_d, scb_gu_d[GK + kc:GK + kc + 1, :]
        else:
            src_q, wdst, scsrc = dq, wd_d, scb_dn_d[kc:kc + 1, :]
        qe = gp if which == "d" else sy
        pk_sems = s_pkg if which == "d" else s_pk
        scr_sems = s_scrg if which == "d" else s_scr
        wwr_sems = s_wwrg if which == "d" else s_wwr
        pslot = dqs["pkslot"] % 3
        dqs["pkslot"] += 1
        if dqs["pkf"][pslot] > 0:
            qe.wait_ge(s_dve.h, dqs["pkf"][pslot])
        pk_sems[pslot].inc(qe.dma_start(
            out=pk_sb[pslot][:, 0:nw],
            in_=src_q[kc * 128:(kc + 1) * 128, col0 // 8:col0 // 8 + nw]), 16)
        dqs["pk_evt"] = (pk_sems[pslot].h, pk_sems[pslot].n)
        bslot = dqs["sbcslot"] % 3
        dqs["sbcslot"] += 1
        if u < 3:
            qe.wait_ge(s_scw.h, 32)
        if u >= 3:
            qe.wait_ge(s_dve.h, dqs.get("mdone_u%d" % (u - 3), 0))
            ws, wv = dqs["wwrval_u%d" % (u - 3)]
            qe.wait_ge(ws, wv)
        scr_sems[bslot].inc(qe.dma_start(
            out=sbc_sb[bslot][:, 0:ncol].rearrange("p (a c) -> p a c", a=1),
            in_=scsrc[:, col0:col0 + ncol].partition_broadcast(128)), 16)
        dqs["scr_evt"] = (scr_sems[bslot].h, scr_sems[bslot].n)
        v.wait_ge(*dqs["pk_evt"])
        if u >= 2:
            v.wait_ge(s_dve.h, dqs["mdone"][slot])
        pk16 = pk_sb[pslot][:, 0:nw].bitcast(mybir.dt.int16)
        nibv = nib_sb[slot][:, 0:8 * nw].rearrange("p (c h) -> p c h", h=8)
        for a in range(4):
            dst = nibv[:, :, 2 * a:2 * a + 2]
            s_dve.inc(v.tensor_scalar(
                dst, pk16[:, 0:2 * nw],
                4 * a, 15, OP.logical_shift_right, OP.bitwise_and))
        v.wait_ge(s_dve.h, s_dve.n)
        v.wait_ge(*dqs["scr_evt"])
        s_dve.inc(v.tensor_tensor(sbc_sb[bslot][:, 0:ncol], nib_sb[slot][:, 0:ncol],
                                  sbc_sb[bslot][:, 0:ncol], OP.mult))
        dqs["mdone"][slot] = s_dve.n
        dqs["mdone_u%d" % u] = s_dve.n
        dqs["pkf"][pslot] = s_dve.n
        qw = gp if which == "d" else sc
        qw.wait_ge(s_dve.h, s_dve.n)
        nblk = ncol // 256
        blk0 = col0 // 256
        dst = wdst[blk0:blk0 + nblk, kc].rearrange("b p c -> p b c")
        srcv = sbc_sb[bslot][:, 0:ncol].rearrange("p (b c) -> p b c", c=256)
        wwr_sems[bslot].inc(qw.dma_start(out=dst, in_=srcv), 16)
        dqs["wwrval_u%d" % u] = (wwr_sems[bslot].h, wwr_sems[bslot].n)
        if u == N_GU - 1:
            dqs["gu_wwr_totals"] = [s.n for s in s_wwr]

    def emit_dequant(n):
        i0 = dqs["i"]
        for i in range(i0, min(i0 + n, N_ALL)):
            dequant_unit(*DQ_UNITS[i])
        dqs["i"] = min(i0 + n, N_ALL)

    aph = ExitStack()
    AE = aph.enter_context
    ln1T = AE(nc.sbuf_tensor([128, 32 * GRP], BF16))
    PT = AE(nc.sbuf_tensor([128, 16 * GRP], BF16))
    x0_t = AE(nc.sbuf_tensor([128, D], F32))

    x0_loads = 0
    evac_done = [0, 0, 0, 0]
    fld_val = {}
    gat_n = 0
    gat_free = [0, 0]
    rowb_used = [0]
    x1_writes = 0
    l3_writes = 0
    x0_free_evt = None
    last_evac = 0
    fpair_done = {}
    cuse_done = {}
    cstop_done = {}
    cread_done = {}

    ga = ExitStack()
    ln1b_t = ga.enter_context(nc.sbuf_tensor([128, D], BF16))
    for rt in range(NRT):
        r0 = rt * 128
        if x0_free_evt is None:
            sy.wait_ge(s_dve.h, P0_DONE)
        else:
            sy.wait_ge(x0_free_evt[0], x0_free_evt[1])
        s_x0.inc(sy.dma_start(out=x0_t[:], in_=x_in[r0:r0 + 128, :]), 16)
        x0_loads += 1
        v.wait_ge(s_x0.h, s_x0.n)
        ln_stats(x0_t)
        sc.wait_ge(s_dve.h, s_dve.n)
        if rt > 0:
            sc.wait_ge(s_zz.h, s_zz.n)
        ins = sc.activation(ln1b_t[:], x0_t[:], AF.Identity,
                            bias=nmrs_t[:], scale=rs_t[:])
        s_scl.inc(ins)
        x0_free_evt = (s_scl.h, s_scl.n)
        sc.wait_ge(s_scl.h, s_scl.n)
        s_zz.inc(sc.dma_start(out=zz_d[r0:r0 + 128, 0:D], in_=ln1b_t[:]), 16)
        s_zz.inc(sc.dma_start(out=zz_d[r0:r0 + 128, D:2 * D], in_=ln1b_t[:]), 16)
    ga.close()

    for g in range(NG):
        if g > 0:
            sy.wait_ge(s_dve.h, s_dve.n)
            sy.wait_ge(s_scl.h, s_scl.n)
            sy.wait_ge(s_x1w[0].h, s_x1w[0].n)
            sy.wait_ge(s_l3w.h, s_l3w.n)
        sy.wait_ge(s_zz.h, s_zz.n)
        for tc in range(32):
            s_l1t.inc(sy.dma_start(
                out=ln1T[:, tc * GRP:(tc + 1) * GRP],
                in_=zz_d[g * GRP:(g + 1) * GRP, tc * 128:(tc + 1) * 128],
                transpose=True), 16)

        ga = ExitStack()
        fcb = [ga.enter_context(nc.sbuf_tensor(f"fcb{g}_{i}", [128, 4096], BF16)) for i in range(3)]
        fsb = [ga.enter_context(nc.sbuf_tensor(f"fsb{g}_{i}", [128, 4096], BF16)) for i in range(3)]
        sq1 = ga.enter_context(nc.sbuf_tensor(f"sq1_{g}", [128, 512], F32))
        sq2 = ga.enter_context(nc.sbuf_tensor(f"sq2_{g}", [128, 512], F32))
        def emit_fload(fb2):
            if not (0 <= fb2 < 16):
                return
            slot2 = fb2 % 3
            pair2 = g * 16 + fb2
            if pair2 >= 3:
                sy.wait_ge(s_pe.h, fpair_done[pair2 - 3])
            s_fld[slot2].inc(sy.dma_start(out=fcb[slot2][:], in_=fc_d[fb2]), 16)
            s_fld[slot2].inc(sy.dma_start(out=fsb[slot2][:], in_=fs_d[fb2]), 16)
            fld_val[fb2] = (s_fld[slot2].h, s_fld[slot2].n)
        emit_fload(0)
        emit_fload(1)
        for fb in range(16):
            slot = fb % 3
            pair = g * 16 + fb
            emit_fload(fb + 2)
            bz_c, bz_s = 2 * (fb % 4), 2 * (fb % 4) + 1
            fh, fv = fld_val[fb]
            te.wait_ge(fh, fv)
            te.wait_ge(s_l1t.h, s_l1t.n)
            if pair >= 4:
                te.wait_ge(s_dve.h, evac_done[fb % 4])
            if g > 0 and fb < 8:
                te.wait_ge(s_dve.h, cread_done[(7, NRTG - 1)])
            for tc in range(32):
                te.matmul(ps[bz_c][:, 0:GRP], fcb[slot][:, tc * 128:(tc + 1) * 128],
                          ln1T[:, tc * GRP:(tc + 1) * GRP],
                          start=(tc == 0), stop=(tc == 31))
                mm = te.matmul(ps[bz_s][:, 0:GRP], fsb[slot][:, tc * 128:(tc + 1) * 128],
                               ln1T[:, tc * GRP:(tc + 1) * GRP],
                               start=(tc == 0), stop=(tc == 31))
                if tc == 31:
                    s_pe.inc(mm)
            fpair_done[pair] = s_pe.n
            sc.wait_ge(s_pe.h, s_pe.n)
            sc.wait_ge(s_dve.h, last_evac)
            s_scl.inc(sc.activation(sq1[:, 0:GRP], ps[bz_c][:, 0:GRP], AF.Square))
            s_scl.inc(sc.activation(sq2[:, 0:GRP], ps[bz_s][:, 0:GRP], AF.Square))
            v.wait_ge(s_scl.h, s_scl.n)
            s_dve.inc(v.tensor_tensor(PT[:, fb * GRP:(fb + 1) * GRP],
                                      sq1[:, 0:GRP], sq2[:, 0:GRP], OP.add))
            evac_done[fb % 4] = s_dve.n
            last_evac = s_dve.n
            emit_dequant(5)
        ga.close()
        pt_done = s_dve.n

        ga = ExitStack()
        ccs = [ga.enter_context(nc.sbuf_tensor(f"ccs{g}_{i}", [128, 512], BF16)) for i in range(2)]
        acc_t = ga.enter_context(nc.sbuf_tensor([128, D], F32))
        rolls = [ga.enter_context(nc.sbuf_tensor(f"roll{g}_{i}", [128, D], BF16)) for i in range(2)]
        x1_t = ga.enter_context(nc.sbuf_tensor([128, D], F32))
        ln3b_t = ga.enter_context(nc.sbuf_tensor([128, D], BF16))
        w8s = ga.enter_context(nc.sbuf_tensor([128, 8 * NRTG], F32))
        idxs = ga.enter_context(nc.sbuf_tensor([128, 8 * NRTG], I32))

        for fs_ in s_fld:
            sy.wait_ge(fs_.h, fs_.n)
        sy.wait_ge(s_pe.h, fpair_done[g * 16 + 15])
        for lc in range(8):
            for fb in range(16):
                cn = g * 128 + lc * 16 + fb
                cslot = cn % 2
                if cn >= 2:
                    sy.wait_ge(s_pe.h, cuse_done[cn - 2])
                s_cld[cslot].inc(sy.dma_start(
                    out=ccs[cslot][:],
                    in_=ccos_d[fb * 128:(fb + 1) * 128, lc * 512:(lc + 1) * 512]), 16)
                te.wait_ge(s_cld[cslot].h, s_cld[cslot].n)
                if fb == 0:
                    te.wait_ge(s_dve.h, pt_done)
                    if lc >= 1:
                        te.wait_ge(s_dve.h, cread_done[(lc - 1, NRTG - 1)])
                for rtl in range(NRTG):
                    last = te.matmul(
                        ps[rtl][:, 0:512],
                        PT[:, fb * GRP + rtl * 128:fb * GRP + (rtl + 1) * 128],
                        ccs[cslot][:], start=(fb == 0), stop=(fb == 15))
                    if fb == 15 or rtl == NRTG - 1:
                        s_pe.inc(last)
                    if fb == 15:
                        cstop_done[(lc, rtl)] = s_pe.n
                cuse_done[cn] = s_pe.n
            for rtl in range(NRTG):
                v.wait_ge(s_pe.h, cstop_done[(lc, rtl)])
                o8 = stkv[:, rtl * 64 + lc * 8: rtl * 64 + lc * 8 + 8]
                s_dve.inc(v.max(out=o8, in_=ps[rtl][:, 0:512]))
                v.wait_ge(s_dve.h, s_dve.n)
                s_dve.inc(v.max_index(out=tmpi8[:], in_max=o8,
                                      in_values=ps[rtl][:, 0:512]))
                v.wait_ge(s_dve.h, s_dve.n)
                ins = v.tensor_scalar(stki[:, rtl * 64 + lc * 8: rtl * 64 + lc * 8 + 8],
                                      tmpi8[:], float(512 * lc), None, OP.add)
                s_dve.inc(ins)
                cread_done[(lc, rtl)] = s_dve.n
            emit_dequant(8)

        for rtl in range(NRTG):
            rt = g * NRTG + rtl
            r0 = rt * 128
            sv = stkv[:, rtl * 64:(rtl + 1) * 64]
            si = stki[:, rtl * 64:(rtl + 1) * 64]
            v.wait_ge(s_dve.h, s_dve.n)
            v.wait_ge(s_scl.h, s_scl.n)
            s_dve.inc(v.max(out=g8[:], in_=sv))
            v.wait_ge(s_dve.h, s_dve.n)
            s_dve.inc(v.tensor_scalar_mul(negmax[:], g8[:, 0:1], -1.0))
            sc.wait_ge(s_dve.h, s_dve.n)
            sc.wait_ge(s_scl.h, s_scl.n)
            s_scl.inc(sc.activation(exp8[:], g8[:], AF.Exp, bias=negmax[:], scale=1.0))
            v.wait_ge(s_scl.h, s_scl.n)
            s_dve.inc(v.reduce_sum(sume[:], exp8[:], axis=AX.X))
            v.wait_ge(s_dve.h, s_dve.n)
            s_dve.inc(v.reciprocal(rsum[:], sume[:]))
            sc.wait_ge(s_dve.h, s_dve.n)
            s_scl.inc(sc.activation(w8s[:, rtl * 8:rtl * 8 + 8], exp8[:], AF.Identity,
                                    bias=0.0, scale=rsum[:]))
            for j in range(8):
                s_dve.inc(v.tensor_scalar(mask64[:], sv, g8[:, j:j + 1], None,
                                          OP.is_equal))
                v.wait_ge(s_dve.h, s_dve.n)
                s_dve.inc(v.tensor_mul(cand64[:], mask64[:], si))
                v.wait_ge(s_dve.h, s_dve.n)
                s_dve.inc(v.reduce_max(d8f[:, j:j + 1], cand64[:], axis=AX.X))
            gp.wait_ge(s_dve.h, rowb_used[0])
            s_gps.inc(gp.iota(rowb_i[:], pattern=[[0, 1]], base=r0 * 2 * D,
                              channel_multiplier=2 * D))
            v.wait_ge(s_gps.h, s_gps.n)
            s_dve.inc(v.tensor_copy(rowb_f[:], rowb_i[:]))
            rowb_used[0] = s_dve.n
            v.wait_ge(s_dve.h, s_dve.n)
            s_dve.inc(v.tensor_scalar(idxf[:], d8f[:], rowb_f[:, 0:1], None, OP.add))
            v.wait_ge(s_dve.h, s_dve.n)
            s_dve.inc(v.tensor_copy(idxs[:, rtl * 8:rtl * 8 + 8], idxf[:]))
        IDX_DONE = s_dve.n
        W8_DONE = s_scl.n

        gp.wait_ge(s_dve.h, IDX_DONE)
        for rtl in range(NRTG):
            rt = g * NRTG + rtl
            r0 = rt * 128
            s_dve.inc(v.memset(acc_t[:], 0.0))
            for k in range(8):
                gslot = gat_n % 2
                if gat_n >= 2:
                    gp.wait_ge(s_dve.h, gat_free[gslot])
                s_gat[gslot].inc(gp.indirect_dma_start(
                    out=rolls[gslot][:], out_offset=None,
                    in_=zz_d[0:R, :],
                    in_offset=IndirectOffsetOnAxis(idxs[:, rtl * 8 + k:rtl * 8 + k + 1], 1)), 16)
                v.wait_ge(s_gat[gslot].h, s_gat[gslot].n)
                v.wait_ge(s_scl.h, W8_DONE)
                v.wait_ge(s_dve.h, s_dve.n)
                ins = v.scalar_tensor_tensor(acc_t[:], rolls[gslot][:],
                                             w8s[:, rtl * 8 + k:rtl * 8 + k + 1],
                                             acc_t[:], OP.mult, OP.add)
                s_dve.inc(ins)
                gat_free[gslot] = s_dve.n
                gat_n += 1
            ln_stats(acc_t)
            sy.wait_ge(x0_free_evt[0], x0_free_evt[1])
            s_x0.inc(sy.dma_start(out=x0_t[:], in_=x_in[r0:r0 + 128, :]), 16)
            x0_loads += 1
            sc.wait_ge(s_dve.h, s_dve.n)
            if x1_writes > 0:
                sc.wait_ge(s_x1w[0].h, s_x1w[0].n)
            s_scl.inc(sc.activation(x1_t[:], acc_t[:], AF.Identity,
                                    bias=nmrs_t[:], scale=rs_t[:]))
            v.wait_ge(s_scl.h, s_scl.n)
            v.wait_ge(s_x0.h, s_x0.n)
            ins = v.tensor_tensor(x1_t[:], x1_t[:], x0_t[:], OP.add)
            s_dve.inc(ins)
            x0_free_evt = (s_dve.h, s_dve.n)
            ln_stats(x1_t)
            sc.wait_ge(s_dve.h, s_dve.n)
            if l3_writes > 0:
                sc.wait_ge(s_l3w.h, s_l3w.n)
            s_scl.inc(sc.activation(ln3b_t[:], x1_t[:], AF.Identity,
                                    bias=nmrs_t[:], scale=rs_t[:]))
            sc.wait_ge(s_scl.h, s_scl.n)
            s_x1w[0].inc(sc.dma_start(out=x1_d[r0:r0 + 128, :], in_=x1_t[:]), 16)
            x1_writes += 1
            s_l3w.inc(sc.dma_start(out=ln3_d[r0:r0 + 128, :], in_=ln3b_t[:]), 16)
            l3_writes += 1
            emit_dequant(4)
        ga.close()

    aph.close()

    bph = ExitStack()
    BE = bph.enter_context
    zs_g = BE(nc.sbuf_tensor([GK, IDIM], BF16))
    zs_u = BE(nc.sbuf_tensor([GK, IDIM], BF16))
    g32_sb = BE(nc.sbuf_tensor([128, GK * GK], BF16))
    sln3 = BE(nc.sbuf_tensor([GK, R], BF16))

    zb = ExitStack()
    znib = zb.enter_context(nc.sbuf_tensor([GI, IDIM], mybir.dt.int16))
    zq_gu = zb.enter_context(nc.sbuf_tensor([64, GUW], I32))
    zq_dn = zb.enter_context(nc.sbuf_tensor([GI, DW], I32))
    sy.wait_ge(s_x1w[0].h, s_x1w[0].n)
    sy.wait_ge(s_l3w.h, s_l3w.n)
    for dst, src in ((zq_gu[0:GK, :], gz), (zq_gu[GK:64, :], uz), (zq_dn[:], dz)):
        s_set.inc(sy.dma_start(out=dst, in_=src[:]), 16)
    v.wait_ge(s_set.h, s_set.n)
    v.wait_ge(s_x1w[0].h, s_x1w[0].n)
    v.wait_ge(s_l3w.h, s_l3w.n)
    zq16 = zq_gu[:].bitcast(mybir.dt.int16)
    for j in range(8):
        pos = POS_OF_SHIFT[j]
        s_dve.inc(v.tensor_scalar(znib[0:64, pos::8], zq16[:, j // 4::2],
                                  4 * (j % 4), 15,
                                  OP.logical_shift_right, OP.bitwise_and))
    v.wait_ge(s_dve.h, s_dve.n)
    s_dve.inc(v.scalar_tensor_tensor(zs_g[:], znib[0:GK, :], -1.0,
                                     sc_gu[0:GK, :], OP.mult, OP.mult))
    s_dve.inc(v.scalar_tensor_tensor(zs_u[:], znib[GK:64, :], -1.0,
                                     sc_gu[GK:64, :], OP.mult, OP.mult))
    v.wait_ge(s_dve.h, s_dve.n)
    zq16d = zq_dn[:].bitcast(mybir.dt.int16)
    for j in range(8):
        pos = POS_OF_SHIFT[j]
        s_dve.inc(v.tensor_scalar(znib[0:GI, pos:pos + 8 * (DW - 1) + 1:8],
                                  zq16d[:, j // 4::2], 4 * (j % 4), 15,
                                  OP.logical_shift_right, OP.bitwise_and))
    v.wait_ge(s_dve.h, s_dve.n)
    s_dve.inc(v.scalar_tensor_tensor(zs_dn[:], znib[0:GI, 0:D], -1.0,
                                     sc_dn[:], OP.mult, OP.mult))
    ZTAB_DONE = s_dve.n
    zb.close()
    ln3T = BE(nc.sbuf_tensor([128, 32 * R], BF16))
    wgub = [BE(nc.sbuf_tensor(f"wgub{i}", [128, 8 * 256], BF16)) for i in range(3)]
    silu_t = [BE(nc.sbuf_tensor(f"silu{i}", [128, 512], BF16)) for i in range(2)]
    h_t = [BE(nc.sbuf_tensor(f"ht{i}", [128, 512], BF16)) for i in range(2)]

    sy.wait_ge(s_l3w.h, s_l3w.n)
    sy.wait_ge(s_x1w[0].h, s_x1w[0].n)
    sy.wait_ge(s_dve.h, ZTAB_DONE)
    s_set.inc(sy.dma_start(out=g32_sb[:], in_=g32_d[:]), 16)
    for tc in range(32):
        s_l3t.inc(sy.dma_start(out=ln3T[:, tc * R:(tc + 1) * R],
                               in_=ln3_d[0:R, tc * 128:(tc + 1) * 128],
                               transpose=True), 16)
    te.wait_ge(s_l3t.h, s_l3t.n)
    te.wait_ge(s_dve.h, cread_done[(7, NRTG - 1)])
    te.wait_ge(s_set.h, s_set.n)
    for t in range(NTC):
        for kc in range(GK):
            mm = te.matmul(ps[6 + t][0:GK, 0:TCS],
                           g32_sb[:, kc * GK:(kc + 1) * GK],
                           ln3T[:, kc * R + t * TCS: kc * R + (t + 1) * TCS],
                           start=(kc == 0), stop=(kc == GK - 1))
            if kc == GK - 1:
                s_pe.inc(mm)
        v.wait_ge(s_pe.h, s_pe.n)
        s_dve.inc(v.tensor_copy(sln3[:, t * TCS:(t + 1) * TCS], ps[6 + t][0:GK, 0:TCS]))
    SLN3_DONE = s_dve.n

    emit_dequant(N_GU - dqs["i"] if dqs["i"] < N_GU else 0)
    for ws, wv in zip(s_wwr, dqs["gu_wwr_totals"]):
        te.wait_ge(ws.h, wv)
    te.wait_ge(s_dve.h, SLN3_DONE)
    hw_n = 0
    wq_n = 0
    wgu_done = {}
    bank_free = {}
    silu_used = [0, 0]
    ffnfree_scl = 0
    ffnfree_dve = 0
    for icb in range(43):
        for q in range(8):
            wslot = wq_n % 3
            if wq_n >= 3:
                sy.wait_ge(s_pe.h, wgu_done[wq_n - 3])
            if wq_n == 0:
                sy.wait_ge(s_l3w.h, s_l3w.n)
                for ws, wv in zip(s_wwr, dqs["gu_wwr_totals"]):
                    sy.wait_ge(ws.h, wv)
            w_i = q % 2
            kq = q // 2
            src = (wg_d if w_i == 0 else wu_d)[icb, kq * 8:(kq + 1) * 8]
            s_wgu[wslot].inc(sy.dma_start(
                out=wgub[wslot][:, 0:8 * 256].rearrange("p (k c) -> p k c", c=256),
                in_=src.rearrange("k p c -> p k c")), 16)
            wq_n += 1
            te.wait_ge(s_wgu[wslot].h, s_wgu[wslot].n)
            for kk in range(8):
                kc = kq * 8 + kk
                for ic2 in range(2):
                    lhs = wgub[wslot][:, kk * 256 + ic2 * 128: kk * 256 + (ic2 + 1) * 128]
                    for t in range(NTC):
                        bank = 4 * w_i + 2 * ic2 + t
                        if kc == 0 and icb >= 1:
                            fs, fv = bank_free[bank]
                            te.wait_ge(fs, fv)
                        last = te.matmul(ps[bank][:, 0:TCS], lhs,
                                  ln3T[:, kc * R + t * TCS: kc * R + (t + 1) * TCS],
                                  start=(kc == 0), stop=False)
            s_pe.inc(last)
            wgu_done[wq_n - 1] = s_pe.n
        for ic2 in range(2):
            ic = icb * 2 + ic2
            for t in range(NTC):
                bg = 2 * ic2 + t
                bu = 4 + 2 * ic2 + t
                mm = te.matmul(ps[bg][:, 0:TCS],
                               zs_g[:, ic * 128:(ic + 1) * 128],
                               sln3[:, t * TCS:(t + 1) * TCS],
                               start=False, stop=True)
                s_pe.inc(mm)
                zg = s_pe.n
                mm = te.matmul(ps[bu][:, 0:TCS],
                               zs_u[:, ic * 128:(ic + 1) * 128],
                               sln3[:, t * TCS:(t + 1) * TCS],
                               start=False, stop=True)
                s_pe.inc(mm)
                zu = s_pe.n
                ssl = hw_n % 2
                sc.wait_ge(s_pe.h, zg)
                if hw_n >= 2:
                    sc.wait_ge(s_dve.h, silu_used[ssl])
                ins = sc.activation(silu_t[ssl][:, 0:TCS], ps[bg][:, 0:TCS], AF.Silu)
                s_scl.inc(ins)
                bank_free[bg] = (s_scl.h, s_scl.n)
                hslot = hw_n % 2
                if hw_n >= 2:
                    v.wait_ge(s_hw[hslot].h, 16 * (hw_n // 2))
                v.wait_ge(s_scl.h, s_scl.n)
                v.wait_ge(s_pe.h, zu)
                ins = v.tensor_tensor(h_t[hslot][:, 0:TCS], silu_t[ssl][:, 0:TCS],
                                      ps[bu][:, 0:TCS], OP.mult)
                s_dve.inc(ins)
                silu_used[ssl] = s_dve.n
                bank_free[bu] = (s_dve.h, s_dve.n)
                gp.wait_ge(s_dve.h, s_dve.n)
                s_hw[hslot].inc(gp.dma_start(
                    out=h_d[ic, :, t * TCS:(t + 1) * TCS],
                    in_=h_t[hslot][:, 0:TCS]), 16)
                hw_n += 1
        ffnfree_scl = s_scl.n
        ffnfree_dve = s_dve.n
        emit_dequant(4)
    emit_dequant(N_ALL)
    bph.close()
    s1.close()

    dph = ExitStack()
    DE = dph.enter_context
    h_sb = DE(nc.sbuf_tensor([128, GI * R], BF16))
    gph = ExitStack()
    g86_sb = gph.enter_context(nc.sbuf_tensor([128, GI * GI], BF16))

    sy.wait_ge(s_scl.h, ffnfree_scl)
    sy.wait_ge(s_dve.h, ffnfree_dve)
    sy.wait_ge(s_hw[0].h, s_hw[0].n)
    sy.wait_ge(s_hw[1].h, s_hw[1].n)
    for ws in s_wwr + s_wwrg:
        sy.wait_ge(ws.h, ws.n)
    s_set.inc(sy.dma_start(out=g86_sb[:], in_=g86_d[:]), 16)
    for ic in range(GI):
        s_hld.inc(sy.dma_start(out=h_sb[:, ic * R:(ic + 1) * R], in_=h_d[ic]), 16)
    te.wait_ge(s_hld.h, s_hld.n)
    te.wait_ge(s_set.h, s_set.n)
    for t in range(NTC):
        for ic in range(GI):
            mm = te.matmul(ps[t][0:GI, 0:TCS], g86_sb[:, ic * GI:(ic + 1) * GI],
                           h_sb[:, ic * R + t * TCS: ic * R + (t + 1) * TCS],
                           start=(ic == 0), stop=(ic == GI - 1))
            if ic == GI - 1:
                s_pe.inc(mm)
        v.wait_ge(s_pe.h, s_pe.n)
        s_dve.inc(v.tensor_copy(hg_sb[:, t * TCS:(t + 1) * TCS], ps[t][0:GI, 0:TCS]))
    HG_DONE = s_dve.n
    gph.close()

    wd_sb = [DE(nc.sbuf_tensor(f"wdsb{i}", [128, 8 * 256], BF16)) for i in range(3)]
    fT_t = [DE(nc.sbuf_tensor(f"fTt{i}", [128, 512], BF16)) for i in range(2)]
    SLABS = []
    for dchb in range(16):
        i0 = 0
        while i0 < GI:
            nic = min(8, GI - i0)
            SLABS.append((dchb, i0, nic))
            i0 += nic
    fT_n = 0
    wd_done = {}
    dfree_done = {}
    for si, (dchb, i0, nic) in enumerate(SLABS):
        wsl = si % 3
        if si >= 3:
            sy.wait_ge(s_pe.h, wd_done[si - 3])
        if si == 0:
            sy.wait_ge(s_dve.h, HG_DONE)
        s_wdl[wsl].inc(sy.dma_start(
            out=wd_sb[wsl][:, 0:nic * 256].rearrange("p (i c) -> p i c", c=256),
            in_=wd_d[dchb, i0:i0 + nic].rearrange("i p c -> p i c")), 16)
        te.wait_ge(s_wdl[wsl].h, s_wdl[wsl].n)
        if i0 == 0:
            if dchb == 0:
                te.wait_ge(s_dve.h, HG_DONE)
            if dchb >= 2:
                te.wait_ge(s_scl.h, dfree_done[dchb - 2])
        last = None
        for ii in range(nic):
            ic = i0 + ii
            for dch2 in range(2):
                lhs = wd_sb[wsl][:, ii * 256 + dch2 * 128: ii * 256 + (dch2 + 1) * 128]
                for t in range(NTC):
                    bank = 4 * (dchb % 2) + 2 * dch2 + t
                    last = te.matmul(ps[bank][:, 0:TCS], lhs,
                                     h_sb[:, ic * R + t * TCS: ic * R + (t + 1) * TCS],
                                     start=(ic == 0), stop=False)
        s_pe.inc(last)
        wd_done[si] = s_pe.n
        if i0 + nic == GI:
            for dch2 in range(2):
                dch = 2 * dchb + dch2
                for t in range(NTC):
                    bank = 4 * (dchb % 2) + 2 * dch2 + t
                    mm = te.matmul(ps[bank][:, 0:TCS],
                                   zs_dn[:, dch * 128:(dch + 1) * 128],
                                   hg_sb[:, t * TCS:(t + 1) * TCS],
                                   start=False, stop=True)
                    s_pe.inc(mm)
            pe_done = s_pe.n
            for dch2 in range(2):
                dch = 2 * dchb + dch2
                for t in range(NTC):
                    bank = 4 * (dchb % 2) + 2 * dch2 + t
                    fsl = fT_n % 2
                    if fT_n >= 2:
                        sc.wait_ge(s_fT[fsl].h, 16 * (fT_n // 2))
                    sc.wait_ge(s_pe.h, pe_done)
                    ins = sc.activation(fT_t[fsl][:, 0:TCS], ps[bank][:, 0:TCS],
                                        AF.Identity, bias=0.0, scale=1.0)
                    s_scl.inc(ins)
                    sc.wait_ge(s_scl.h, s_scl.n)
                    s_fT[fsl].inc(sc.dma_start(
                        out=ffnT_d[dch * 128:(dch + 1) * 128, t * TCS:(t + 1) * TCS],
                        in_=fT_t[fsl][:, 0:TCS]), 16)
                    fT_n += 1
            dfree_done[dchb] = s_scl.n
    dph.close()

    pph = ExitStack()
    PE_ = pph.enter_context
    ftok = [PE_(nc.sbuf_tensor(f"ftok{i}", [128, D], BF16)) for i in range(2)]
    x1l_t = [PE_(nc.sbuf_tensor(f"x1l{i}", [128, D], F32)) for i in range(2)]
    ln4n_t = PE_(nc.sbuf_tensor([128, D], F32))
    out_t = [PE_(nc.sbuf_tensor(f"outt{i}", [128, D], F32)) for i in range(2)]

    sy.wait_ge(s_fT[0].h, s_fT[0].n)
    sy.wait_ge(s_fT[1].h, s_fT[1].n)
    sy.wait_ge(s_x1w[0].h, s_x1w[0].n)
    p4_free = {}
    for rt in range(NRT):
        r0 = rt * 128
        sl = rt % 2
        if rt >= 2:
            sy.wait_ge(s_dve.h, p4_free[rt - 2])
        s_fTl[sl].inc(sy.dma_start(out=ftok[sl][:], in_=ffnT_d[0:D, r0:r0 + 128],
                                   transpose=True), 16)
        s_x1l[sl].inc(sy.dma_start(out=x1l_t[sl][:], in_=x1_d[r0:r0 + 128, :]), 16)
        v.wait_ge(s_fTl[sl].h, s_fTl[sl].n)
        ln_stats(ftok[sl])
        sc.wait_ge(s_dve.h, s_dve.n)
        s_scl.inc(sc.activation(ln4n_t[:], ftok[sl][:], AF.Identity,
                                bias=nmrs_t[:], scale=rs_t[:]))
        v.wait_ge(s_scl.h, s_scl.n)
        v.wait_ge(s_x1l[sl].h, s_x1l[sl].n)
        if rt >= 2:
            v.wait_ge(s_out[sl].h, 16 * (rt // 2))
        ins = v.tensor_add(out_t[sl][:], ln4n_t[:], x1l_t[sl][:])
        s_dve.inc(ins)
        p4_free[rt] = s_dve.n
        sy.wait_ge(s_dve.h, s_dve.n)
        s_out[sl].inc(sy.dma_start(out=out_d[r0:r0 + 128, :], in_=out_t[sl][:]), 16)
    sy.wait_ge(s_out[0].h, s_out[0].n)
    sy.wait_ge(s_out[1].h, s_out[1].n)

    pph.close()
    ctx.close()
    return nc


_PROG_CACHE = {}


def _get_prog(R):
    if R not in _PROG_CACHE:
        _PROG_CACHE[R] = build_program(R)
    return _PROG_CACHE[R]


def kernel(L0_attn_x, L0_ffn_gate_qweight, L0_ffn_gate_scales, L0_ffn_gate_qzeros,
           L0_ffn_up_qweight, L0_ffn_up_scales, L0_ffn_up_qzeros,
           L0_ffn_down_qweight, L0_ffn_down_scales, L0_ffn_down_qzeros):
    x = np.ascontiguousarray(np.asarray(L0_attn_x, dtype=np.float32)).reshape(8192, D)
    R = 8192 // 8
    cst = _consts()
    nc = _get_prog(R)
    shared = dict(
        gq=np.ascontiguousarray(np.asarray(L0_ffn_gate_qweight, dtype=np.int32)),
        gs=np.ascontiguousarray(np.asarray(L0_ffn_gate_scales, dtype=np.float32)),
        gz=np.ascontiguousarray(np.asarray(L0_ffn_gate_qzeros, dtype=np.int32)),
        uq=np.ascontiguousarray(np.asarray(L0_ffn_up_qweight, dtype=np.int32)),
        us=np.ascontiguousarray(np.asarray(L0_ffn_up_scales, dtype=np.float32)),
        uz=np.ascontiguousarray(np.asarray(L0_ffn_up_qzeros, dtype=np.int32)),
        dq=np.ascontiguousarray(np.asarray(L0_ffn_down_qweight, dtype=np.int32)),
        ds=np.ascontiguousarray(np.asarray(L0_ffn_down_scales, dtype=np.float32)),
        dz=np.ascontiguousarray(np.asarray(L0_ffn_down_qzeros, dtype=np.int32)),
        fc4=cst["fc4"], fs4=cst["fs4"], ccos=cst["ccos"],
        g32=cst["g32"], g86=cst["g86"])
    in_maps = [dict(shared, x=np.ascontiguousarray(x[c * R:(c + 1) * R]))
               for c in range(8)]
    res = run_bass_kernel_spmd(nc, in_maps, core_ids=list(range(8)))
    out = np.concatenate([np.asarray(r["out"], dtype=np.float32)
                          for r in res.results], axis=0)
    return out.reshape(4, 2048, D)


# version 15
# speedup vs baseline: 1.0090x; 1.0069x over previous
import math
import numpy as np
import ml_dtypes
import concourse.bass as bass
import concourse.mybir as mybir
from concourse.bass import IndirectOffsetOnAxis
from concourse.bass_utils import run_bass_kernel_spmd
from contextlib import ExitStack

F32 = mybir.dt.float32
BF16 = mybir.dt.bfloat16
I32 = mybir.dt.int32
UI32 = mybir.dt.uint32
AF = mybir.ActivationFunctionType
OP = mybir.AluOpType
AX = mybir.AxisListType

D = 4096
IDIM = 11008
NF = 2048
GK = 32
GI = 86
EPS = 1e-5
GUW = IDIM // 8
DW = D // 8
GU_SLICES = [(0, 2816), (2816, 2816), (5632, 2816), (8448, 2560)]
DN_SLICES = [(0, 2048), (2048, 2048)]
POS_OF_SHIFT = [0, 2, 4, 6, 1, 3, 5, 7]


def _bf(x):
    return np.asarray(x, dtype=ml_dtypes.bfloat16)


_CONST = {}


def _consts():
    if not _CONST:
        t = np.arange(D, dtype=np.int64)
        f = np.arange(1, NF + 1, dtype=np.int64)
        ph = ((t[:, None] * f[None, :]) % D).astype(np.float64) * (2 * math.pi / D)
        fc4 = _bf(np.cos(ph)).reshape(32, 128, 16, 128).transpose(2, 1, 0, 3).copy()
        fs4 = _bf(np.sin(ph)).reshape(32, 128, 16, 128).transpose(2, 1, 0, 3).copy()
        ll = np.arange(D, dtype=np.int64)
        phc = ((f[:, None] * ll[None, :]) % D).astype(np.float64) * (2 * math.pi / D)
        cc = np.cos(phc) * (2.0 / D)
        cc[NF - 1, :] *= 0.5
        g32 = (np.arange(D)[:, None] // 128 == np.arange(GK)[None, :]).astype(np.float32)
        g86 = (np.arange(IDIM)[:, None] // 128 == np.arange(GI)[None, :]).astype(np.float32)
        _CONST.update(
            fc4=fc4, fs4=fs4, ccos=_bf(cc).copy(),
            g32=_bf(g32).reshape(GK, 128, GK).transpose(1, 0, 2).copy(),
            g86=_bf(g86).reshape(GI, 128, GI).transpose(1, 0, 2).copy())
    return _CONST


class Sem:
    __slots__ = ("h", "n")

    def __init__(self, ctx, nc, name):
        self.h = ctx.enter_context(nc.semaphore(name))
        self.n = 0

    def inc(self, instr, k=1):
        instr.then_inc(self.h, k)
        self.n += k
        return self.n


def build_program(R):
    NRT = R // 128
    GRP = min(512, R)
    NG = R // GRP
    NRTG = GRP // 128
    TCS = min(512, R)
    NTC = R // TCS

    nc = bass.Bass("TRN2", target_bir_lowering=False)

    x_in = nc.dram_tensor("x", [R, D], F32, kind="ExternalInput").ap()
    out_d = nc.dram_tensor("out", [R, D], F32, kind="ExternalOutput").ap()
    gq = nc.dram_tensor("gq", [D, GUW], I32, kind="ExternalInput").ap()
    gs = nc.dram_tensor("gs", [GK, IDIM], F32, kind="ExternalInput").ap()
    gz = nc.dram_tensor("gz", [GK, GUW], I32, kind="ExternalInput").ap()
    uq = nc.dram_tensor("uq", [D, GUW], I32, kind="ExternalInput").ap()
    us = nc.dram_tensor("us", [GK, IDIM], F32, kind="ExternalInput").ap()
    uz = nc.dram_tensor("uz", [GK, GUW], I32, kind="ExternalInput").ap()
    dq = nc.dram_tensor("dq", [IDIM, DW], I32, kind="ExternalInput").ap()
    ds = nc.dram_tensor("ds", [GI, D], F32, kind="ExternalInput").ap()
    dz = nc.dram_tensor("dz", [GI, DW], I32, kind="ExternalInput").ap()
    fc_d = nc.dram_tensor("fc4", [16, 128, 32, 128], BF16, kind="ExternalInput").ap()
    fs_d = nc.dram_tensor("fs4", [16, 128, 32, 128], BF16, kind="ExternalInput").ap()
    ccos_d = nc.dram_tensor("ccos", [NF, D], BF16, kind="ExternalInput").ap()
    g32_d = nc.dram_tensor("g32", [128, GK, GK], BF16, kind="ExternalInput").ap()
    g86_d = nc.dram_tensor("g86", [128, GI, GI], BF16, kind="ExternalInput").ap()

    zz_d = nc.dram_tensor("zz_s", [R, 2 * D], BF16, kind="Internal").ap()
    x1_d = nc.dram_tensor("x1_s", [R, D], F32, kind="Internal").ap()
    ln3_d = nc.dram_tensor("ln3_s", [R, D], BF16, kind="Internal").ap()
    wg_d = nc.dram_tensor("wg_s", [43, GK, 128, 256], BF16, kind="Internal").ap()
    wu_d = nc.dram_tensor("wu_s", [43, GK, 128, 256], BF16, kind="Internal").ap()
    wd_d = nc.dram_tensor("wd_s", [16, GI, 128, 256], BF16, kind="Internal").ap()
    h_d = nc.dram_tensor("h_s", [GI, 128, R], BF16, kind="Internal").ap()
    ffnT_d = nc.dram_tensor("ffnT_s", [D, R], BF16, kind="Internal").ap()
    scb_gu_d = nc.dram_tensor("scbgu_s", [64, IDIM], BF16, kind="Internal").ap()
    scb_dn_d = nc.dram_tensor("scbdn_s", [GI, D], BF16, kind="Internal").ap()

    ctx = ExitStack()
    E = ctx.enter_context
    SS = lambda name: Sem(ctx, nc, name)

    ps = [E(nc.psum_tensor(f"psb{i}", [128, 512], F32)) for i in range(8)]

    mv = E(nc.sbuf_tensor([128, 2], F32))
    bnst = E(nc.sbuf_tensor([128, 48], F32))
    std_t = E(nc.sbuf_tensor([128, 1], F32))
    rs_t = E(nc.sbuf_tensor([128, 1], F32))
    nmrs_t = E(nc.sbuf_tensor([128, 1], F32))
    stkv = E(nc.sbuf_tensor([128, 64 * 4], F32))
    stki = E(nc.sbuf_tensor([128, 64 * 4], F32))
    tmpi8 = E(nc.sbuf_tensor([128, 8], UI32))
    g8 = E(nc.sbuf_tensor([128, 8], F32))
    negmax = E(nc.sbuf_tensor([128, 1], F32))
    exp8 = E(nc.sbuf_tensor([128, 8], F32))
    sume = E(nc.sbuf_tensor([128, 1], F32))
    rsum = E(nc.sbuf_tensor([128, 1], F32))
    w8 = E(nc.sbuf_tensor([128, 8], F32))
    mask64 = E(nc.sbuf_tensor([128, 64], F32))
    cand64 = E(nc.sbuf_tensor([128, 64], F32))
    d8f = E(nc.sbuf_tensor([128, 8], F32))
    idxf = E(nc.sbuf_tensor([128, 8], F32))
    idx_i = E(nc.sbuf_tensor([128, 8], I32))
    rowb_i = E(nc.sbuf_tensor([128, 1], I32))
    rowb_f = E(nc.sbuf_tensor([128, 1], F32))
    zs_dn = E(nc.sbuf_tensor([GI, D], BF16))
    eps_t = E(nc.sbuf_tensor([128, 1], F32))
    hg_sb = E(nc.sbuf_tensor([GI, R], BF16))

    s_set = SS("set")
    s_dve = SS("dve")
    s_scl = SS("scl")
    s_gps = SS("gps")
    s_pe = SS("pe")
    s_x0 = SS("x0")
    s_x0f = SS("x0f")
    s_zz = SS("zz")
    s_l1t = SS("l1t")
    s_fld = [SS(f"fld{i}") for i in range(3)]
    s_ffree = SS("ffree")
    s_cld = [SS(f"cld{i}") for i in range(2)]
    s_cuse = SS("cuse")
    s_cstop = SS("cstop")
    s_cread = SS("cread")
    s_pk = [SS(f"pk{i}") for i in range(3)]
    s_pkg = [SS(f"pkg{i}") for i in range(3)]
    s_scrg = [SS(f"scrg{i}") for i in range(3)]
    s_wwrg = [SS(f"wwrg{i}") for i in range(3)]
    s_scr = [SS(f"scr{i}") for i in range(3)]
    s_scw = SS("scw")
    s_wwr = [SS(f"wwr{i}") for i in range(3)]
    s_gat = [SS("gat0"), SS("gat1")]
    s_gatf = SS("gatf")
    s_x1w = [SS("x1w0"), SS("x1w1")]
    s_l3w = SS("l3w")
    s_l3t = SS("l3t")
    s_wgu = [SS(f"wgu{i}") for i in range(3)]
    s_wguf = SS("wguf")
    s_hw = [SS("hw0"), SS("hw1")]
    s_ffnfree = SS("ffnfree")
    s_hld = SS("hld")
    s_wdl = [SS(f"wdl{i}") for i in range(3)]
    s_wdf = SS("wdf")
    s_dfree = SS("dfree")
    s_fT = [SS("fT0"), SS("fT1")]
    s_fTl = [SS("fTl0"), SS("fTl1")]
    s_x1l = [SS("x1l0"), SS("x1l1")]
    s_p4f = SS("p4f")
    s_out = [SS("out0"), SS("out1")]

    blk = E(nc.Block())
    te, v, sc, gp, sy = nc.tensor, nc.vector, nc.scalar, nc.gpsimd, nc.sync

    s1 = ExitStack()
    S1 = s1.enter_context
    sc_gu = S1(nc.sbuf_tensor([64, IDIM], BF16))
    sc_dn = S1(nc.sbuf_tensor([GI, D], BF16))
    pk_sb = [S1(nc.sbuf_tensor(f"pk{i}", [128, 352], I32)) for i in range(3)]
    nib_sb = [S1(nc.sbuf_tensor(f"nib{i}", [128, 2816], mybir.dt.int16)) for i in range(2)]
    sbc_sb = [S1(nc.sbuf_tensor(f"sbc{i}", [128, 2816], BF16)) for i in range(3)]

    s_dve.inc(v.memset(eps_t[:], EPS))
    p0 = ExitStack()
    sc32_gu = p0.enter_context(nc.sbuf_tensor([64, IDIM], F32))
    sc32_dn = p0.enter_context(nc.sbuf_tensor([GI, D], F32))

    for dst, src in ((sc32_gu[0:GK, :], gs), (sc32_gu[GK:64, :], us),
                     (sc32_dn[:], ds)):
        s_set.inc(sy.dma_start(out=dst, in_=src[:]), 16)
    v.wait_ge(s_set.h, s_set.n)
    s_dve.inc(v.tensor_copy(sc_gu[:], sc32_gu[:]))
    s_dve.inc(v.tensor_copy(sc_dn[:], sc32_dn[:]))
    P0_DONE = s_dve.n
    sc.wait_ge(s_dve.h, P0_DONE)
    s_scw.inc(sc.dma_start(out=scb_gu_d[:], in_=sc_gu[:]), 16)
    s_scw.inc(sc.dma_start(out=scb_dn_d[:], in_=sc_dn[:]), 16)
    p0.close()

    def ln_stats(src_ap, fd=D):
        nch = max(1, fd // 512)
        cw = fd // nch
        v.wait_ge(s_dve.h, s_dve.n)
        for c in range(nch):
            s_dve.inc(v.bn_stats(bnst[:, 6 * c:6 * c + 6],
                                 src_ap[:, c * cw:(c + 1) * cw]))
        v.wait_ge(s_dve.h, s_dve.n)
        s_dve.inc(v.bn_aggr(mv[:], bnst[:, 0:6 * nch]))
        sc.wait_ge(s_dve.h, s_dve.n)
        s_scl.inc(sc.activation(std_t[:], mv[:, 1:2], AF.Sqrt, bias=eps_t[:], scale=1.0))
        v.wait_ge(s_scl.h, s_scl.n)
        s_dve.inc(v.reciprocal(rs_t[:], std_t[:]))
        v.wait_ge(s_dve.h, s_dve.n)
        s_dve.inc(v.scalar_tensor_tensor(nmrs_t[:], mv[:, 0:1], -1.0, rs_t[:],
                                         OP.mult, OP.mult))
        return s_dve.n

    DQ_UNITS = ([("g", kc, sl) for kc in range(GK) for sl in range(4)]
                + [("u", kc, sl) for kc in range(GK) for sl in range(4)]
                + [("d", ic, sl) for ic in range(GI) for sl in range(2)])
    N_GU = 256
    N_ALL = len(DQ_UNITS)
    dqs = {"u": 0, "i": 0, "mdone": [0, 0], "pkslot": 0,
           "pkf": [0, 0, 0], "sbcslot": 0}

    def dequant_unit(which, kc, sl):
        u = dqs["u"]
        dqs["u"] += 1
        slot = u % 2
        col0, ncol = (GU_SLICES[sl] if which != "d" else DN_SLICES[sl])
        nw = ncol // 8
        if which == "g":
            src_q, wdst, scsrc = gq, wg_d, scb_gu_d[kc:kc + 1, :]
        elif which == "u":
            src_q, wdst, scsrc = uq, wu_d, scb_gu_d[GK + kc:GK + kc + 1, :]
        else:
            src_q, wdst, scsrc = dq, wd_d, scb_dn_d[kc:kc + 1, :]
        qe = gp if which == "d" else sy
        pk_sems = s_pkg if which == "d" else s_pk
        scr_sems = s_scrg if which == "d" else s_scr
        wwr_sems = s_wwrg if which == "d" else s_wwr
        pslot = dqs["pkslot"] % 3
        dqs["pkslot"] += 1
        if dqs["pkf"][pslot] > 0:
            qe.wait_ge(s_dve.h, dqs["pkf"][pslot])
        pk_sems[pslot].inc(qe.dma_start(
            out=pk_sb[pslot][:, 0:nw],
            in_=src_q[kc * 128:(kc + 1) * 128, col0 // 8:col0 // 8 + nw]), 16)
        dqs["pk_evt"] = (pk_sems[pslot].h, pk_sems[pslot].n)
        bslot = dqs["sbcslot"] % 3
        dqs["sbcslot"] += 1
        if u < 3:
            qe.wait_ge(s_scw.h, 32)
        if u >= 3:
            qe.wait_ge(s_dve.h, dqs.get("mdone_u%d" % (u - 3), 0))
            ws, wv = dqs["wwrval_u%d" % (u - 3)]
            qe.wait_ge(ws, wv)
        scr_sems[bslot].inc(qe.dma_start(
            out=sbc_sb[bslot][:, 0:ncol].rearrange("p (a c) -> p a c", a=1),
            in_=scsrc[:, col0:col0 + ncol].partition_broadcast(128)), 16)
        dqs["scr_evt"] = (scr_sems[bslot].h, scr_sems[bslot].n)
        v.wait_ge(*dqs["pk_evt"])
        if u >= 2:
            v.wait_ge(s_dve.h, dqs["mdone"][slot])
        pk16 = pk_sb[pslot][:, 0:nw].bitcast(mybir.dt.int16)
        nibv = nib_sb[slot][:, 0:8 * nw].rearrange("p (c h) -> p c h", h=8)
        for a in range(4):
            dst = nibv[:, :, 2 * a:2 * a + 2]
            s_dve.inc(v.tensor_scalar(
                dst, pk16[:, 0:2 * nw],
                4 * a, 15, OP.logical_shift_right, OP.bitwise_and))
        v.wait_ge(s_dve.h, s_dve.n)
        v.wait_ge(*dqs["scr_evt"])
        s_dve.inc(v.tensor_tensor(sbc_sb[bslot][:, 0:ncol], nib_sb[slot][:, 0:ncol],
                                  sbc_sb[bslot][:, 0:ncol], OP.mult))
        dqs["mdone"][slot] = s_dve.n
        dqs["mdone_u%d" % u] = s_dve.n
        dqs["pkf"][pslot] = s_dve.n
        qw = gp if which == "d" else sc
        qw.wait_ge(s_dve.h, s_dve.n)
        nblk = ncol // 256
        blk0 = col0 // 256
        dst = wdst[blk0:blk0 + nblk, kc].rearrange("b p c -> p b c")
        srcv = sbc_sb[bslot][:, 0:ncol].rearrange("p (b c) -> p b c", c=256)
        wwr_sems[bslot].inc(qw.dma_start(out=dst, in_=srcv), 16)
        dqs["wwrval_u%d" % u] = (wwr_sems[bslot].h, wwr_sems[bslot].n)
        if u == N_GU - 1:
            dqs["gu_wwr_totals"] = [s.n for s in s_wwr]

    def emit_dequant(n):
        i0 = dqs["i"]
        for i in range(i0, min(i0 + n, N_ALL)):
            dequant_unit(*DQ_UNITS[i])
        dqs["i"] = min(i0 + n, N_ALL)

    aph = ExitStack()
    AE = aph.enter_context
    ln1T = AE(nc.sbuf_tensor([128, 32 * GRP], BF16))
    PT = AE(nc.sbuf_tensor([128, 16 * GRP], BF16))
    x0_t = AE(nc.sbuf_tensor([128, D], F32))

    x0_loads = 0
    evac_done = [0, 0, 0, 0]
    fld_val = {}
    gat_n = 0
    gat_free = [0, 0]
    rowb_used = [0]
    x1_writes = 0
    l3_writes = 0
    x0_free_evt = None
    last_evac = 0
    fpair_done = {}
    cuse_done = {}
    cstop_done = {}
    cread_done = {}

    ga = ExitStack()
    ln1b_t = ga.enter_context(nc.sbuf_tensor([128, D], BF16))
    for rt in range(NRT):
        r0 = rt * 128
        if x0_free_evt is None:
            sy.wait_ge(s_dve.h, P0_DONE)
        else:
            sy.wait_ge(x0_free_evt[0], x0_free_evt[1])
        s_x0.inc(sy.dma_start(out=x0_t[:], in_=x_in[r0:r0 + 128, :]), 16)
        x0_loads += 1
        v.wait_ge(s_x0.h, s_x0.n)
        ln_stats(x0_t)
        sc.wait_ge(s_dve.h, s_dve.n)
        if rt > 0:
            sc.wait_ge(s_zz.h, s_zz.n)
        ins = sc.activation(ln1b_t[:], x0_t[:], AF.Identity,
                            bias=nmrs_t[:], scale=rs_t[:])
        s_scl.inc(ins)
        x0_free_evt = (s_scl.h, s_scl.n)
        sc.wait_ge(s_scl.h, s_scl.n)
        s_zz.inc(sc.dma_start(out=zz_d[r0:r0 + 128, 0:D], in_=ln1b_t[:]), 16)
        s_zz.inc(sc.dma_start(out=zz_d[r0:r0 + 128, D:2 * D], in_=ln1b_t[:]), 16)
    ga.close()

    for g in range(NG):
        if g > 0:
            sy.wait_ge(s_dve.h, s_dve.n)
            sy.wait_ge(s_scl.h, s_scl.n)
            sy.wait_ge(s_x1w[0].h, s_x1w[0].n)
            sy.wait_ge(s_l3w.h, s_l3w.n)
        sy.wait_ge(s_zz.h, s_zz.n)
        for tc in range(32):
            s_l1t.inc(sy.dma_start(
                out=ln1T[:, tc * GRP:(tc + 1) * GRP],
                in_=zz_d[g * GRP:(g + 1) * GRP, tc * 128:(tc + 1) * 128],
                transpose=True), 16)

        ga = ExitStack()
        fcb = [ga.enter_context(nc.sbuf_tensor(f"fcb{g}_{i}", [128, 4096], BF16)) for i in range(3)]
        fsb = [ga.enter_context(nc.sbuf_tensor(f"fsb{g}_{i}", [128, 4096], BF16)) for i in range(3)]
        sq1 = ga.enter_context(nc.sbuf_tensor(f"sq1_{g}", [128, 512], F32))
        sq2 = ga.enter_context(nc.sbuf_tensor(f"sq2_{g}", [128, 512], F32))
        def emit_fload(fb2):
            if not (0 <= fb2 < 16):
                return
            slot2 = fb2 % 3
            pair2 = g * 16 + fb2
            if pair2 >= 3:
                sy.wait_ge(s_pe.h, fpair_done[pair2 - 3])
            s_fld[slot2].inc(sy.dma_start(out=fcb[slot2][:], in_=fc_d[fb2]), 16)
            s_fld[slot2].inc(sy.dma_start(out=fsb[slot2][:], in_=fs_d[fb2]), 16)
            fld_val[fb2] = (s_fld[slot2].h, s_fld[slot2].n)
        emit_fload(0)
        emit_fload(1)
        for fb in range(16):
            slot = fb % 3
            pair = g * 16 + fb
            emit_fload(fb + 2)
            bz_c, bz_s = 2 * (fb % 4), 2 * (fb % 4) + 1
            fh, fv = fld_val[fb]
            te.wait_ge(fh, fv)
            te.wait_ge(s_l1t.h, s_l1t.n)
            if pair >= 4:
                te.wait_ge(s_dve.h, evac_done[fb % 4])
            if g > 0 and fb < 8:
                te.wait_ge(s_dve.h, cread_done[(7, NRTG - 1)])
            for tc in range(32):
                te.matmul(ps[bz_c][:, 0:GRP], fcb[slot][:, tc * 128:(tc + 1) * 128],
                          ln1T[:, tc * GRP:(tc + 1) * GRP],
                          start=(tc == 0), stop=(tc == 31))
                mm = te.matmul(ps[bz_s][:, 0:GRP], fsb[slot][:, tc * 128:(tc + 1) * 128],
                               ln1T[:, tc * GRP:(tc + 1) * GRP],
                               start=(tc == 0), stop=(tc == 31))
                if tc == 31:
                    s_pe.inc(mm)
            fpair_done[pair] = s_pe.n
            sc.wait_ge(s_pe.h, s_pe.n)
            sc.wait_ge(s_dve.h, last_evac)
            s_scl.inc(sc.activation(sq1[:, 0:GRP], ps[bz_c][:, 0:GRP], AF.Square))
            s_scl.inc(sc.activation(sq2[:, 0:GRP], ps[bz_s][:, 0:GRP], AF.Square))
            v.wait_ge(s_scl.h, s_scl.n)
            s_dve.inc(v.tensor_tensor(PT[:, fb * GRP:(fb + 1) * GRP],
                                      sq1[:, 0:GRP], sq2[:, 0:GRP], OP.add))
            evac_done[fb % 4] = s_dve.n
            last_evac = s_dve.n
            emit_dequant(4)
        ga.close()
        pt_done = s_dve.n

        ga = ExitStack()
        ccs = [ga.enter_context(nc.sbuf_tensor(f"ccs{g}_{i}", [128, 512], BF16)) for i in range(2)]
        acc_t = ga.enter_context(nc.sbuf_tensor([128, D], F32))
        rolls = [ga.enter_context(nc.sbuf_tensor(f"roll{g}_{i}", [128, D], BF16)) for i in range(2)]
        x1_t = ga.enter_context(nc.sbuf_tensor([128, D], F32))
        ln3b_t = ga.enter_context(nc.sbuf_tensor([128, D], BF16))
        w8s = ga.enter_context(nc.sbuf_tensor([128, 8 * NRTG], F32))
        idxs = ga.enter_context(nc.sbuf_tensor([128, 8 * NRTG], I32))

        for fs_ in s_fld:
            sy.wait_ge(fs_.h, fs_.n)
        sy.wait_ge(s_pe.h, fpair_done[g * 16 + 15])
        for lc in range(8):
            for fb in range(16):
                cn = g * 128 + lc * 16 + fb
                cslot = cn % 2
                if cn >= 2:
                    sy.wait_ge(s_pe.h, cuse_done[cn - 2])
                s_cld[cslot].inc(sy.dma_start(
                    out=ccs[cslot][:],
                    in_=ccos_d[fb * 128:(fb + 1) * 128, lc * 512:(lc + 1) * 512]), 16)
                te.wait_ge(s_cld[cslot].h, s_cld[cslot].n)
                if fb == 0:
                    te.wait_ge(s_dve.h, pt_done)
                    if lc >= 1:
                        te.wait_ge(s_dve.h, cread_done[(lc - 1, NRTG - 1)])
                for rtl in range(NRTG):
                    last = te.matmul(
                        ps[rtl][:, 0:512],
                        PT[:, fb * GRP + rtl * 128:fb * GRP + (rtl + 1) * 128],
                        ccs[cslot][:], start=(fb == 0), stop=(fb == 15))
                    if fb == 15 or rtl == NRTG - 1:
                        s_pe.inc(last)
                    if fb == 15:
                        cstop_done[(lc, rtl)] = s_pe.n
                cuse_done[cn] = s_pe.n
            for rtl in range(NRTG):
                v.wait_ge(s_pe.h, cstop_done[(lc, rtl)])
                o8 = stkv[:, rtl * 64 + lc * 8: rtl * 64 + lc * 8 + 8]
                s_dve.inc(v.max(out=o8, in_=ps[rtl][:, 0:512]))
                v.wait_ge(s_dve.h, s_dve.n)
                s_dve.inc(v.max_index(out=tmpi8[:], in_max=o8,
                                      in_values=ps[rtl][:, 0:512]))
                v.wait_ge(s_dve.h, s_dve.n)
                ins = v.tensor_scalar(stki[:, rtl * 64 + lc * 8: rtl * 64 + lc * 8 + 8],
                                      tmpi8[:], float(512 * lc), None, OP.add)
                s_dve.inc(ins)
                cread_done[(lc, rtl)] = s_dve.n
            emit_dequant(10)

        for rtl in range(NRTG):
            rt = g * NRTG + rtl
            r0 = rt * 128
            sv = stkv[:, rtl * 64:(rtl + 1) * 64]
            si = stki[:, rtl * 64:(rtl + 1) * 64]
            v.wait_ge(s_dve.h, s_dve.n)
            v.wait_ge(s_scl.h, s_scl.n)
            s_dve.inc(v.max(out=g8[:], in_=sv))
            v.wait_ge(s_dve.h, s_dve.n)
            s_dve.inc(v.tensor_scalar_mul(negmax[:], g8[:, 0:1], -1.0))
            sc.wait_ge(s_dve.h, s_dve.n)
            sc.wait_ge(s_scl.h, s_scl.n)
            s_scl.inc(sc.activation(exp8[:], g8[:], AF.Exp, bias=negmax[:], scale=1.0))
            v.wait_ge(s_scl.h, s_scl.n)
            s_dve.inc(v.reduce_sum(sume[:], exp8[:], axis=AX.X))
            v.wait_ge(s_dve.h, s_dve.n)
            s_dve.inc(v.reciprocal(rsum[:], sume[:]))
            sc.wait_ge(s_dve.h, s_dve.n)
            s_scl.inc(sc.activation(w8s[:, rtl * 8:rtl * 8 + 8], exp8[:], AF.Identity,
                                    bias=0.0, scale=rsum[:]))
            for j in range(8):
                s_dve.inc(v.tensor_scalar(mask64[:], sv, g8[:, j:j + 1], None,
                                          OP.is_equal))
                v.wait_ge(s_dve.h, s_dve.n)
                s_dve.inc(v.tensor_mul(cand64[:], mask64[:], si))
                v.wait_ge(s_dve.h, s_dve.n)
                s_dve.inc(v.reduce_max(d8f[:, j:j + 1], cand64[:], axis=AX.X))
            gp.wait_ge(s_dve.h, rowb_used[0])
            s_gps.inc(gp.iota(rowb_i[:], pattern=[[0, 1]], base=r0 * 2 * D,
                              channel_multiplier=2 * D))
            v.wait_ge(s_gps.h, s_gps.n)
            s_dve.inc(v.tensor_copy(rowb_f[:], rowb_i[:]))
            rowb_used[0] = s_dve.n
            v.wait_ge(s_dve.h, s_dve.n)
            s_dve.inc(v.tensor_scalar(idxf[:], d8f[:], rowb_f[:, 0:1], None, OP.add))
            v.wait_ge(s_dve.h, s_dve.n)
            s_dve.inc(v.tensor_copy(idxs[:, rtl * 8:rtl * 8 + 8], idxf[:]))
        IDX_DONE = s_dve.n
        W8_DONE = s_scl.n

        gp.wait_ge(s_dve.h, IDX_DONE)
        for rtl in range(NRTG):
            rt = g * NRTG + rtl
            r0 = rt * 128
            s_dve.inc(v.memset(acc_t[:], 0.0))
            for k in range(8):
                gslot = gat_n % 2
                if gat_n >= 2:
                    gp.wait_ge(s_dve.h, gat_free[gslot])
                s_gat[gslot].inc(gp.indirect_dma_start(
                    out=rolls[gslot][:], out_offset=None,
                    in_=zz_d[0:R, :],
                    in_offset=IndirectOffsetOnAxis(idxs[:, rtl * 8 + k:rtl * 8 + k + 1], 1)), 16)
                v.wait_ge(s_gat[gslot].h, s_gat[gslot].n)
                v.wait_ge(s_scl.h, W8_DONE)
                v.wait_ge(s_dve.h, s_dve.n)
                ins = v.scalar_tensor_tensor(acc_t[:], rolls[gslot][:],
                                             w8s[:, rtl * 8 + k:rtl * 8 + k + 1],
                                             acc_t[:], OP.mult, OP.add)
                s_dve.inc(ins)
                gat_free[gslot] = s_dve.n
                gat_n += 1
            ln_stats(acc_t)
            sy.wait_ge(x0_free_evt[0], x0_free_evt[1])
            s_x0.inc(sy.dma_start(out=x0_t[:], in_=x_in[r0:r0 + 128, :]), 16)
            x0_loads += 1
            sc.wait_ge(s_dve.h, s_dve.n)
            if x1_writes > 0:
                sc.wait_ge(s_x1w[0].h, s_x1w[0].n)
            s_scl.inc(sc.activation(x1_t[:], acc_t[:], AF.Identity,
                                    bias=nmrs_t[:], scale=rs_t[:]))
            v.wait_ge(s_scl.h, s_scl.n)
            v.wait_ge(s_x0.h, s_x0.n)
            ins = v.tensor_tensor(x1_t[:], x1_t[:], x0_t[:], OP.add)
            s_dve.inc(ins)
            x0_free_evt = (s_dve.h, s_dve.n)
            ln_stats(x1_t)
            sc.wait_ge(s_dve.h, s_dve.n)
            if l3_writes > 0:
                sc.wait_ge(s_l3w.h, s_l3w.n)
            s_scl.inc(sc.activation(ln3b_t[:], x1_t[:], AF.Identity,
                                    bias=nmrs_t[:], scale=rs_t[:]))
            sc.wait_ge(s_scl.h, s_scl.n)
            s_x1w[0].inc(sc.dma_start(out=x1_d[r0:r0 + 128, :], in_=x1_t[:]), 16)
            x1_writes += 1
            s_l3w.inc(sc.dma_start(out=ln3_d[r0:r0 + 128, :], in_=ln3b_t[:]), 16)
            l3_writes += 1
            emit_dequant(4)
        ga.close()

    aph.close()

    bph = ExitStack()
    BE = bph.enter_context
    zs_g = BE(nc.sbuf_tensor([GK, IDIM], BF16))
    zs_u = BE(nc.sbuf_tensor([GK, IDIM], BF16))
    g32_sb = BE(nc.sbuf_tensor([128, GK * GK], BF16))
    sln3 = BE(nc.sbuf_tensor([GK, R], BF16))

    zb = ExitStack()
    znib = zb.enter_context(nc.sbuf_tensor([GI, IDIM], mybir.dt.int16))
    zq_gu = zb.enter_context(nc.sbuf_tensor([64, GUW], I32))
    zq_dn = zb.enter_context(nc.sbuf_tensor([GI, DW], I32))
    sy.wait_ge(s_x1w[0].h, s_x1w[0].n)
    sy.wait_ge(s_l3w.h, s_l3w.n)
    for dst, src in ((zq_gu[0:GK, :], gz), (zq_gu[GK:64, :], uz), (zq_dn[:], dz)):
        s_set.inc(sy.dma_start(out=dst, in_=src[:]), 16)
    v.wait_ge(s_set.h, s_set.n)
    v.wait_ge(s_x1w[0].h, s_x1w[0].n)
    v.wait_ge(s_l3w.h, s_l3w.n)
    zq16 = zq_gu[:].bitcast(mybir.dt.int16)
    for j in range(8):
        pos = POS_OF_SHIFT[j]
        s_dve.inc(v.tensor_scalar(znib[0:64, pos::8], zq16[:, j // 4::2],
                                  4 * (j % 4), 15,
                                  OP.logical_shift_right, OP.bitwise_and))
    v.wait_ge(s_dve.h, s_dve.n)
    s_dve.inc(v.scalar_tensor_tensor(zs_g[:], znib[0:GK, :], -1.0,
                                     sc_gu[0:GK, :], OP.mult, OP.mult))
    s_dve.inc(v.scalar_tensor_tensor(zs_u[:], znib[GK:64, :], -1.0,
                                     sc_gu[GK:64, :], OP.mult, OP.mult))
    v.wait_ge(s_dve.h, s_dve.n)
    zq16d = zq_dn[:].bitcast(mybir.dt.int16)
    for j in range(8):
        pos = POS_OF_SHIFT[j]
        s_dve.inc(v.tensor_scalar(znib[0:GI, pos:pos + 8 * (DW - 1) + 1:8],
                                  zq16d[:, j // 4::2], 4 * (j % 4), 15,
                                  OP.logical_shift_right, OP.bitwise_and))
    v.wait_ge(s_dve.h, s_dve.n)
    s_dve.inc(v.scalar_tensor_tensor(zs_dn[:], znib[0:GI, 0:D], -1.0,
                                     sc_dn[:], OP.mult, OP.mult))
    ZTAB_DONE = s_dve.n
    zb.close()
    ln3T = BE(nc.sbuf_tensor([128, 32 * R], BF16))
    wgub = [BE(nc.sbuf_tensor(f"wgub{i}", [128, 8 * 256], BF16)) for i in range(3)]
    silu_t = [BE(nc.sbuf_tensor(f"silu{i}", [128, 512], BF16)) for i in range(2)]
    h_t = [BE(nc.sbuf_tensor(f"ht{i}", [128, 512], BF16)) for i in range(2)]

    sy.wait_ge(s_l3w.h, s_l3w.n)
    sy.wait_ge(s_x1w[0].h, s_x1w[0].n)
    sy.wait_ge(s_dve.h, ZTAB_DONE)
    s_set.inc(sy.dma_start(out=g32_sb[:], in_=g32_d[:]), 16)
    for tc in range(32):
        s_l3t.inc(sy.dma_start(out=ln3T[:, tc * R:(tc + 1) * R],
                               in_=ln3_d[0:R, tc * 128:(tc + 1) * 128],
                               transpose=True), 16)
    te.wait_ge(s_l3t.h, s_l3t.n)
    te.wait_ge(s_dve.h, cread_done[(7, NRTG - 1)])
    te.wait_ge(s_set.h, s_set.n)
    for t in range(NTC):
        for kc in range(GK):
            mm = te.matmul(ps[6 + t][0:GK, 0:TCS],
                           g32_sb[:, kc * GK:(kc + 1) * GK],
                           ln3T[:, kc * R + t * TCS: kc * R + (t + 1) * TCS],
                           start=(kc == 0), stop=(kc == GK - 1))
            if kc == GK - 1:
                s_pe.inc(mm)
        v.wait_ge(s_pe.h, s_pe.n)
        s_dve.inc(v.tensor_copy(sln3[:, t * TCS:(t + 1) * TCS], ps[6 + t][0:GK, 0:TCS]))
    SLN3_DONE = s_dve.n

    emit_dequant(N_GU - dqs["i"] if dqs["i"] < N_GU else 0)
    for ws, wv in zip(s_wwr, dqs["gu_wwr_totals"]):
        te.wait_ge(ws.h, wv)
    te.wait_ge(s_dve.h, SLN3_DONE)
    hw_n = 0
    wq_n = 0
    wgu_done = {}
    bank_free = {}
    silu_used = [0, 0]
    ffnfree_scl = 0
    ffnfree_dve = 0
    for icb in range(43):
        for q in range(8):
            wslot = wq_n % 3
            if wq_n >= 3:
                sy.wait_ge(s_pe.h, wgu_done[wq_n - 3])
            if wq_n == 0:
                sy.wait_ge(s_l3w.h, s_l3w.n)
                for ws, wv in zip(s_wwr, dqs["gu_wwr_totals"]):
                    sy.wait_ge(ws.h, wv)
            w_i = q % 2
            kq = q // 2
            src = (wg_d if w_i == 0 else wu_d)[icb, kq * 8:(kq + 1) * 8]
            s_wgu[wslot].inc(sy.dma_start(
                out=wgub[wslot][:, 0:8 * 256].rearrange("p (k c) -> p k c", c=256),
                in_=src.rearrange("k p c -> p k c")), 16)
            wq_n += 1
            te.wait_ge(s_wgu[wslot].h, s_wgu[wslot].n)
            for kk in range(8):
                kc = kq * 8 + kk
                for ic2 in range(2):
                    lhs = wgub[wslot][:, kk * 256 + ic2 * 128: kk * 256 + (ic2 + 1) * 128]
                    for t in range(NTC):
                        bank = 4 * w_i + 2 * ic2 + t
                        if kc == 0 and icb >= 1:
                            fs, fv = bank_free[bank]
                            te.wait_ge(fs, fv)
                        last = te.matmul(ps[bank][:, 0:TCS], lhs,
                                  ln3T[:, kc * R + t * TCS: kc * R + (t + 1) * TCS],
                                  start=(kc == 0), stop=False)
            s_pe.inc(last)
            wgu_done[wq_n - 1] = s_pe.n
        for ic2 in range(2):
            ic = icb * 2 + ic2
            for t in range(NTC):
                bg = 2 * ic2 + t
                bu = 4 + 2 * ic2 + t
                mm = te.matmul(ps[bg][:, 0:TCS],
                               zs_g[:, ic * 128:(ic + 1) * 128],
                               sln3[:, t * TCS:(t + 1) * TCS],
                               start=False, stop=True)
                s_pe.inc(mm)
                zg = s_pe.n
                mm = te.matmul(ps[bu][:, 0:TCS],
                               zs_u[:, ic * 128:(ic + 1) * 128],
                               sln3[:, t * TCS:(t + 1) * TCS],
                               start=False, stop=True)
                s_pe.inc(mm)
                zu = s_pe.n
                ssl = hw_n % 2
                sc.wait_ge(s_pe.h, zg)
                if hw_n >= 2:
                    sc.wait_ge(s_dve.h, silu_used[ssl])
                ins = sc.activation(silu_t[ssl][:, 0:TCS], ps[bg][:, 0:TCS], AF.Silu)
                s_scl.inc(ins)
                bank_free[bg] = (s_scl.h, s_scl.n)
                hslot = hw_n % 2
                if hw_n >= 2:
                    v.wait_ge(s_hw[hslot].h, 16 * (hw_n // 2))
                v.wait_ge(s_scl.h, s_scl.n)
                v.wait_ge(s_pe.h, zu)
                ins = v.tensor_tensor(h_t[hslot][:, 0:TCS], silu_t[ssl][:, 0:TCS],
                                      ps[bu][:, 0:TCS], OP.mult)
                s_dve.inc(ins)
                silu_used[ssl] = s_dve.n
                bank_free[bu] = (s_dve.h, s_dve.n)
                gp.wait_ge(s_dve.h, s_dve.n)
                s_hw[hslot].inc(gp.dma_start(
                    out=h_d[ic, :, t * TCS:(t + 1) * TCS],
                    in_=h_t[hslot][:, 0:TCS]), 16)
                hw_n += 1
        ffnfree_scl = s_scl.n
        ffnfree_dve = s_dve.n
        emit_dequant(4)
    emit_dequant(N_ALL)
    bph.close()
    s1.close()

    dph = ExitStack()
    DE = dph.enter_context
    h_sb = DE(nc.sbuf_tensor([128, GI * R], BF16))
    gph = ExitStack()
    g86_sb = gph.enter_context(nc.sbuf_tensor([128, GI * GI], BF16))

    sy.wait_ge(s_scl.h, ffnfree_scl)
    sy.wait_ge(s_dve.h, ffnfree_dve)
    sy.wait_ge(s_hw[0].h, s_hw[0].n)
    sy.wait_ge(s_hw[1].h, s_hw[1].n)
    for ws in s_wwr + s_wwrg:
        sy.wait_ge(ws.h, ws.n)
    s_set.inc(sy.dma_start(out=g86_sb[:], in_=g86_d[:]), 16)
    for ic in range(GI):
        s_hld.inc(sy.dma_start(out=h_sb[:, ic * R:(ic + 1) * R], in_=h_d[ic]), 16)
    te.wait_ge(s_hld.h, s_hld.n)
    te.wait_ge(s_set.h, s_set.n)
    for t in range(NTC):
        for ic in range(GI):
            mm = te.matmul(ps[t][0:GI, 0:TCS], g86_sb[:, ic * GI:(ic + 1) * GI],
                           h_sb[:, ic * R + t * TCS: ic * R + (t + 1) * TCS],
                           start=(ic == 0), stop=(ic == GI - 1))
            if ic == GI - 1:
                s_pe.inc(mm)
        v.wait_ge(s_pe.h, s_pe.n)
        s_dve.inc(v.tensor_copy(hg_sb[:, t * TCS:(t + 1) * TCS], ps[t][0:GI, 0:TCS]))
    HG_DONE = s_dve.n
    gph.close()

    wd_sb = [DE(nc.sbuf_tensor(f"wdsb{i}", [128, 8 * 256], BF16)) for i in range(3)]
    fT_t = [DE(nc.sbuf_tensor(f"fTt{i}", [128, 512], BF16)) for i in range(2)]
    SLABS = []
    for dchb in range(16):
        i0 = 0
        while i0 < GI:
            nic = min(8, GI - i0)
            SLABS.append((dchb, i0, nic))
            i0 += nic
    fT_n = 0
    wd_done = {}
    dfree_done = {}
    for si, (dchb, i0, nic) in enumerate(SLABS):
        wsl = si % 3
        if si >= 3:
            sy.wait_ge(s_pe.h, wd_done[si - 3])
        if si == 0:
            sy.wait_ge(s_dve.h, HG_DONE)
        s_wdl[wsl].inc(sy.dma_start(
            out=wd_sb[wsl][:, 0:nic * 256].rearrange("p (i c) -> p i c", c=256),
            in_=wd_d[dchb, i0:i0 + nic].rearrange("i p c -> p i c")), 16)
        te.wait_ge(s_wdl[wsl].h, s_wdl[wsl].n)
        if i0 == 0:
            if dchb == 0:
                te.wait_ge(s_dve.h, HG_DONE)
            if dchb >= 2:
                te.wait_ge(s_scl.h, dfree_done[dchb - 2])
        last = None
        for ii in range(nic):
            ic = i0 + ii
            for dch2 in range(2):
                lhs = wd_sb[wsl][:, ii * 256 + dch2 * 128: ii * 256 + (dch2 + 1) * 128]
                for t in range(NTC):
                    bank = 4 * (dchb % 2) + 2 * dch2 + t
                    last = te.matmul(ps[bank][:, 0:TCS], lhs,
                                     h_sb[:, ic * R + t * TCS: ic * R + (t + 1) * TCS],
                                     start=(ic == 0), stop=False)
        s_pe.inc(last)
        wd_done[si] = s_pe.n
        if i0 + nic == GI:
            for dch2 in range(2):
                dch = 2 * dchb + dch2
                for t in range(NTC):
                    bank = 4 * (dchb % 2) + 2 * dch2 + t
                    mm = te.matmul(ps[bank][:, 0:TCS],
                                   zs_dn[:, dch * 128:(dch + 1) * 128],
                                   hg_sb[:, t * TCS:(t + 1) * TCS],
                                   start=False, stop=True)
                    s_pe.inc(mm)
            pe_done = s_pe.n
            for dch2 in range(2):
                dch = 2 * dchb + dch2
                for t in range(NTC):
                    bank = 4 * (dchb % 2) + 2 * dch2 + t
                    fsl = fT_n % 2
                    if fT_n >= 2:
                        sc.wait_ge(s_fT[fsl].h, 16 * (fT_n // 2))
                    sc.wait_ge(s_pe.h, pe_done)
                    ins = sc.activation(fT_t[fsl][:, 0:TCS], ps[bank][:, 0:TCS],
                                        AF.Identity, bias=0.0, scale=1.0)
                    s_scl.inc(ins)
                    sc.wait_ge(s_scl.h, s_scl.n)
                    s_fT[fsl].inc(sc.dma_start(
                        out=ffnT_d[dch * 128:(dch + 1) * 128, t * TCS:(t + 1) * TCS],
                        in_=fT_t[fsl][:, 0:TCS]), 16)
                    fT_n += 1
            dfree_done[dchb] = s_scl.n
    dph.close()

    pph = ExitStack()
    PE_ = pph.enter_context
    ftok = [PE_(nc.sbuf_tensor(f"ftok{i}", [128, D], BF16)) for i in range(2)]
    x1l_t = [PE_(nc.sbuf_tensor(f"x1l{i}", [128, D], F32)) for i in range(2)]
    ln4n_t = PE_(nc.sbuf_tensor([128, D], F32))
    out_t = [PE_(nc.sbuf_tensor(f"outt{i}", [128, D], F32)) for i in range(2)]

    sy.wait_ge(s_fT[0].h, s_fT[0].n)
    sy.wait_ge(s_fT[1].h, s_fT[1].n)
    sy.wait_ge(s_x1w[0].h, s_x1w[0].n)
    p4_free = {}
    for rt in range(NRT):
        r0 = rt * 128
        sl = rt % 2
        if rt >= 2:
            sy.wait_ge(s_dve.h, p4_free[rt - 2])
        s_fTl[sl].inc(sy.dma_start(out=ftok[sl][:], in_=ffnT_d[0:D, r0:r0 + 128],
                                   transpose=True), 16)
        s_x1l[sl].inc(sy.dma_start(out=x1l_t[sl][:], in_=x1_d[r0:r0 + 128, :]), 16)
        v.wait_ge(s_fTl[sl].h, s_fTl[sl].n)
        ln_stats(ftok[sl])
        sc.wait_ge(s_dve.h, s_dve.n)
        s_scl.inc(sc.activation(ln4n_t[:], ftok[sl][:], AF.Identity,
                                bias=nmrs_t[:], scale=rs_t[:]))
        v.wait_ge(s_scl.h, s_scl.n)
        v.wait_ge(s_x1l[sl].h, s_x1l[sl].n)
        if rt >= 2:
            v.wait_ge(s_out[sl].h, 16 * (rt // 2))
        ins = v.tensor_add(out_t[sl][:], ln4n_t[:], x1l_t[sl][:])
        s_dve.inc(ins)
        p4_free[rt] = s_dve.n
        sy.wait_ge(s_dve.h, s_dve.n)
        s_out[sl].inc(sy.dma_start(out=out_d[r0:r0 + 128, :], in_=out_t[sl][:]), 16)
    sy.wait_ge(s_out[0].h, s_out[0].n)
    sy.wait_ge(s_out[1].h, s_out[1].n)

    pph.close()
    ctx.close()
    return nc


_PROG_CACHE = {}


def _get_prog(R):
    if R not in _PROG_CACHE:
        _PROG_CACHE[R] = build_program(R)
    return _PROG_CACHE[R]


def kernel(L0_attn_x, L0_ffn_gate_qweight, L0_ffn_gate_scales, L0_ffn_gate_qzeros,
           L0_ffn_up_qweight, L0_ffn_up_scales, L0_ffn_up_qzeros,
           L0_ffn_down_qweight, L0_ffn_down_scales, L0_ffn_down_qzeros):
    x = np.ascontiguousarray(np.asarray(L0_attn_x, dtype=np.float32)).reshape(8192, D)
    R = 8192 // 8
    cst = _consts()
    nc = _get_prog(R)
    shared = dict(
        gq=np.ascontiguousarray(np.asarray(L0_ffn_gate_qweight, dtype=np.int32)),
        gs=np.ascontiguousarray(np.asarray(L0_ffn_gate_scales, dtype=np.float32)),
        gz=np.ascontiguousarray(np.asarray(L0_ffn_gate_qzeros, dtype=np.int32)),
        uq=np.ascontiguousarray(np.asarray(L0_ffn_up_qweight, dtype=np.int32)),
        us=np.ascontiguousarray(np.asarray(L0_ffn_up_scales, dtype=np.float32)),
        uz=np.ascontiguousarray(np.asarray(L0_ffn_up_qzeros, dtype=np.int32)),
        dq=np.ascontiguousarray(np.asarray(L0_ffn_down_qweight, dtype=np.int32)),
        ds=np.ascontiguousarray(np.asarray(L0_ffn_down_scales, dtype=np.float32)),
        dz=np.ascontiguousarray(np.asarray(L0_ffn_down_qzeros, dtype=np.int32)),
        fc4=cst["fc4"], fs4=cst["fs4"], ccos=cst["ccos"],
        g32=cst["g32"], g86=cst["g86"])
    in_maps = [dict(shared, x=np.ascontiguousarray(x[c * R:(c + 1) * R]))
               for c in range(8)]
    res = run_bass_kernel_spmd(nc, in_maps, core_ids=list(range(8)))
    out = np.concatenate([np.asarray(r["out"], dtype=np.float32)
                          for r in res.results], axis=0)
    return out.reshape(4, 2048, D)


# version 16 (final)
# speedup vs baseline: 1.0508x; 1.0414x over previous
import math
import numpy as np
import ml_dtypes
import concourse.bass as bass
import concourse.mybir as mybir
from concourse.bass import IndirectOffsetOnAxis
from concourse.bass_utils import run_bass_kernel_spmd
from contextlib import ExitStack

F32 = mybir.dt.float32
BF16 = mybir.dt.bfloat16
I32 = mybir.dt.int32
UI32 = mybir.dt.uint32
AF = mybir.ActivationFunctionType
OP = mybir.AluOpType
AX = mybir.AxisListType

D = 4096
IDIM = 11008
NF = 2048
GK = 32
GI = 86
EPS = 1e-5
GUW = IDIM // 8
DW = D // 8
GU_SLICES = [(0, 2816), (2816, 2816), (5632, 2816), (8448, 2560)]
DN_SLICES = [(0, 2048), (2048, 2048)]
POS_OF_SHIFT = [0, 2, 4, 6, 1, 3, 5, 7]


def _bf(x):
    return np.asarray(x, dtype=ml_dtypes.bfloat16)


_CONST = {}


def _consts():
    if not _CONST:
        t = np.arange(D, dtype=np.int64)
        f = np.arange(1, NF + 1, dtype=np.int64)
        ph = ((t[:, None] * f[None, :]) % D).astype(np.float64) * (2 * math.pi / D)
        fc4 = _bf(np.cos(ph)).reshape(32, 128, 16, 128).transpose(2, 1, 0, 3).copy()
        fs4 = _bf(np.sin(ph)).reshape(32, 128, 16, 128).transpose(2, 1, 0, 3).copy()
        ll = np.arange(D, dtype=np.int64)
        phc = ((f[:, None] * ll[None, :]) % D).astype(np.float64) * (2 * math.pi / D)
        cc = np.cos(phc) * (2.0 / D)
        cc[NF - 1, :] *= 0.5
        g32 = (np.arange(D)[:, None] // 128 == np.arange(GK)[None, :]).astype(np.float32)
        g86 = (np.arange(IDIM)[:, None] // 128 == np.arange(GI)[None, :]).astype(np.float32)
        _CONST.update(
            fc4=fc4, fs4=fs4, ccos=_bf(cc).copy(),
            g32=_bf(g32).reshape(GK, 128, GK).transpose(1, 0, 2).copy(),
            g86=_bf(g86).reshape(GI, 128, GI).transpose(1, 0, 2).copy())
    return _CONST


class Sem:
    __slots__ = ("h", "n")

    def __init__(self, ctx, nc, name):
        self.h = ctx.enter_context(nc.semaphore(name))
        self.n = 0

    def inc(self, instr, k=1):
        instr.then_inc(self.h, k)
        self.n += k
        return self.n


def build_program(R):
    NRT = R // 128
    GRP = min(512, R)
    NG = R // GRP
    NRTG = GRP // 128
    TCS = min(512, R)
    NTC = R // TCS

    nc = bass.Bass("TRN2", target_bir_lowering=False)

    x_in = nc.dram_tensor("x", [R, D], F32, kind="ExternalInput").ap()
    out_d = nc.dram_tensor("out", [R, D], F32, kind="ExternalOutput").ap()
    gq = nc.dram_tensor("gq", [D, GUW], I32, kind="ExternalInput").ap()
    gs = nc.dram_tensor("gs", [GK, IDIM], F32, kind="ExternalInput").ap()
    gz = nc.dram_tensor("gz", [GK, GUW], I32, kind="ExternalInput").ap()
    uq = nc.dram_tensor("uq", [D, GUW], I32, kind="ExternalInput").ap()
    us = nc.dram_tensor("us", [GK, IDIM], F32, kind="ExternalInput").ap()
    uz = nc.dram_tensor("uz", [GK, GUW], I32, kind="ExternalInput").ap()
    dq = nc.dram_tensor("dq", [IDIM, DW], I32, kind="ExternalInput").ap()
    ds = nc.dram_tensor("ds", [GI, D], F32, kind="ExternalInput").ap()
    dz = nc.dram_tensor("dz", [GI, DW], I32, kind="ExternalInput").ap()
    fc_d = nc.dram_tensor("fc4", [16, 128, 32, 128], BF16, kind="ExternalInput").ap()
    fs_d = nc.dram_tensor("fs4", [16, 128, 32, 128], BF16, kind="ExternalInput").ap()
    ccos_d = nc.dram_tensor("ccos", [NF, D], BF16, kind="ExternalInput").ap()
    g32_d = nc.dram_tensor("g32", [128, GK, GK], BF16, kind="ExternalInput").ap()
    g86_d = nc.dram_tensor("g86", [128, GI, GI], BF16, kind="ExternalInput").ap()

    zz_d = nc.dram_tensor("zz_s", [R, 2 * D], BF16, kind="Internal").ap()
    x1_d = nc.dram_tensor("x1_s", [R, D], F32, kind="Internal").ap()
    ln3_d = nc.dram_tensor("ln3_s", [R, D], BF16, kind="Internal").ap()
    wg_d = nc.dram_tensor("wg_s", [43, GK, 128, 256], BF16, kind="Internal").ap()
    wu_d = nc.dram_tensor("wu_s", [43, GK, 128, 256], BF16, kind="Internal").ap()
    wd_d = nc.dram_tensor("wd_s", [16, GI, 128, 256], BF16, kind="Internal").ap()
    h_d = nc.dram_tensor("h_s", [GI, 128, R], BF16, kind="Internal").ap()
    ffnT_d = nc.dram_tensor("ffnT_s", [D, R], BF16, kind="Internal").ap()
    scb_gu_d = nc.dram_tensor("scbgu_s", [64, IDIM], BF16, kind="Internal").ap()
    scb_dn_d = nc.dram_tensor("scbdn_s", [GI, D], BF16, kind="Internal").ap()

    ctx = ExitStack()
    E = ctx.enter_context
    SS = lambda name: Sem(ctx, nc, name)

    ps = [E(nc.psum_tensor(f"psb{i}", [128, 512], F32)) for i in range(8)]

    mv = E(nc.sbuf_tensor([128, 2], F32))
    bnst = E(nc.sbuf_tensor([128, 48], F32))
    std_t = E(nc.sbuf_tensor([128, 1], F32))
    rs_t = E(nc.sbuf_tensor([128, 1], F32))
    nmrs_t = E(nc.sbuf_tensor([128, 1], F32))
    stkv = E(nc.sbuf_tensor([128, 64 * 4], F32))
    stki = E(nc.sbuf_tensor([128, 64 * 4], F32))
    tmpi8 = E(nc.sbuf_tensor([128, 8], UI32))
    g8 = E(nc.sbuf_tensor([128, 8], F32))
    negmax = E(nc.sbuf_tensor([128, 1], F32))
    exp8 = E(nc.sbuf_tensor([128, 8], F32))
    sume = E(nc.sbuf_tensor([128, 1], F32))
    rsum = E(nc.sbuf_tensor([128, 1], F32))
    w8 = E(nc.sbuf_tensor([128, 8], F32))
    mask64 = E(nc.sbuf_tensor([128, 64], F32))
    cand64 = E(nc.sbuf_tensor([128, 64], F32))
    d8f = E(nc.sbuf_tensor([128, 8], F32))
    idxf = E(nc.sbuf_tensor([128, 8], F32))
    idx_i = E(nc.sbuf_tensor([128, 8], I32))
    rowb_i = E(nc.sbuf_tensor([128, 4], I32))
    rowb_f = E(nc.sbuf_tensor([128, 4], F32))
    zs_dn = E(nc.sbuf_tensor([GI, D], BF16))
    eps_t = E(nc.sbuf_tensor([128, 1], F32))
    hg_sb = E(nc.sbuf_tensor([GI, R], BF16))

    s_set = SS("set")
    s_dve = SS("dve")
    s_scl = SS("scl")
    s_gps = SS("gps")
    s_pe = SS("pe")
    s_x0 = SS("x0")
    s_x0f = SS("x0f")
    s_zz = SS("zz")
    s_l1t = SS("l1t")
    s_fld = [SS(f"fld{i}") for i in range(3)]
    s_ffree = SS("ffree")
    s_cld = [SS(f"cld{i}") for i in range(2)]
    s_cuse = SS("cuse")
    s_cstop = SS("cstop")
    s_cread = SS("cread")
    s_pk = [SS(f"pk{i}") for i in range(3)]
    s_pkg = [SS(f"pkg{i}") for i in range(3)]
    s_scrg = [SS(f"scrg{i}") for i in range(3)]
    s_wwrg = [SS(f"wwrg{i}") for i in range(3)]
    s_scr = [SS(f"scr{i}") for i in range(3)]
    s_scw = SS("scw")
    s_wwr = [SS(f"wwr{i}") for i in range(3)]
    s_gat = [SS("gat0"), SS("gat1"), SS("gat2")]
    s_gatf = SS("gatf")
    s_x1w = [SS("x1w0"), SS("x1w1")]
    s_l3w = SS("l3w")
    s_l3t = SS("l3t")
    s_wgu = [SS(f"wgu{i}") for i in range(3)]
    s_wguf = SS("wguf")
    s_hw = [SS("hw0"), SS("hw1")]
    s_ffnfree = SS("ffnfree")
    s_hld = SS("hld")
    s_wdl = [SS(f"wdl{i}") for i in range(3)]
    s_wdf = SS("wdf")
    s_dfree = SS("dfree")
    s_fT = [SS("fT0"), SS("fT1")]
    s_fTl = [SS("fTl0"), SS("fTl1")]
    s_x1l = [SS("x1l0"), SS("x1l1")]
    s_p4f = SS("p4f")
    s_out = [SS("out0"), SS("out1")]

    blk = E(nc.Block())
    te, v, sc, gp, sy = nc.tensor, nc.vector, nc.scalar, nc.gpsimd, nc.sync

    s1 = ExitStack()
    S1 = s1.enter_context
    sc_gu = S1(nc.sbuf_tensor([64, IDIM], BF16))
    sc_dn = S1(nc.sbuf_tensor([GI, D], BF16))
    pk_sb = [S1(nc.sbuf_tensor(f"pk{i}", [128, 352], I32)) for i in range(3)]
    nib_sb = [S1(nc.sbuf_tensor(f"nib{i}", [128, 2816], mybir.dt.int16)) for i in range(2)]
    sbc_sb = [S1(nc.sbuf_tensor(f"sbc{i}", [128, 2816], BF16)) for i in range(3)]

    s_dve.inc(v.memset(eps_t[:], EPS))
    p0 = ExitStack()
    sc32_gu = p0.enter_context(nc.sbuf_tensor([64, IDIM], F32))
    sc32_dn = p0.enter_context(nc.sbuf_tensor([GI, D], F32))

    for dst, src in ((sc32_gu[0:GK, :], gs), (sc32_gu[GK:64, :], us),
                     (sc32_dn[:], ds)):
        s_set.inc(sy.dma_start(out=dst, in_=src[:]), 16)
    v.wait_ge(s_set.h, s_set.n)
    s_dve.inc(v.tensor_copy(sc_gu[:], sc32_gu[:]))
    s_dve.inc(v.tensor_copy(sc_dn[:], sc32_dn[:]))
    P0_DONE = s_dve.n
    sc.wait_ge(s_dve.h, P0_DONE)
    s_scw.inc(sc.dma_start(out=scb_gu_d[:], in_=sc_gu[:]), 16)
    s_scw.inc(sc.dma_start(out=scb_dn_d[:], in_=sc_dn[:]), 16)
    p0.close()

    def ln_stats(src_ap, fd=D):
        nch = max(1, fd // 512)
        cw = fd // nch
        v.wait_ge(s_dve.h, s_dve.n)
        for c in range(nch):
            s_dve.inc(v.bn_stats(bnst[:, 6 * c:6 * c + 6],
                                 src_ap[:, c * cw:(c + 1) * cw]))
        v.wait_ge(s_dve.h, s_dve.n)
        s_dve.inc(v.bn_aggr(mv[:], bnst[:, 0:6 * nch]))
        sc.wait_ge(s_dve.h, s_dve.n)
        s_scl.inc(sc.activation(std_t[:], mv[:, 1:2], AF.Sqrt, bias=eps_t[:], scale=1.0))
        v.wait_ge(s_scl.h, s_scl.n)
        s_dve.inc(v.reciprocal(rs_t[:], std_t[:]))
        v.wait_ge(s_dve.h, s_dve.n)
        s_dve.inc(v.scalar_tensor_tensor(nmrs_t[:], mv[:, 0:1], -1.0, rs_t[:],
                                         OP.mult, OP.mult))
        return s_dve.n

    DQ_UNITS = ([("g", kc, sl) for kc in range(GK) for sl in range(4)]
                + [("u", kc, sl) for kc in range(GK) for sl in range(4)]
                + [("d", ic, sl) for ic in range(GI) for sl in range(2)])
    N_GU = 256
    N_ALL = len(DQ_UNITS)
    dqs = {"u": 0, "i": 0, "mdone": [0, 0], "pkslot": 0,
           "pkf": [0, 0, 0], "sbcslot": 0}

    def dequant_unit(which, kc, sl):
        u = dqs["u"]
        dqs["u"] += 1
        slot = u % 2
        col0, ncol = (GU_SLICES[sl] if which != "d" else DN_SLICES[sl])
        nw = ncol // 8
        if which == "g":
            src_q, wdst, scsrc = gq, wg_d, scb_gu_d[kc:kc + 1, :]
        elif which == "u":
            src_q, wdst, scsrc = uq, wu_d, scb_gu_d[GK + kc:GK + kc + 1, :]
        else:
            src_q, wdst, scsrc = dq, wd_d, scb_dn_d[kc:kc + 1, :]
        qe = gp if which == "d" else sy
        pk_sems = s_pkg if which == "d" else s_pk
        scr_sems = s_scrg if which == "d" else s_scr
        wwr_sems = s_wwrg if which == "d" else s_wwr
        pslot = dqs["pkslot"] % 3
        dqs["pkslot"] += 1
        if dqs["pkf"][pslot] > 0:
            qe.wait_ge(s_dve.h, dqs["pkf"][pslot])
        pk_sems[pslot].inc(qe.dma_start(
            out=pk_sb[pslot][:, 0:nw],
            in_=src_q[kc * 128:(kc + 1) * 128, col0 // 8:col0 // 8 + nw]), 16)
        dqs["pk_evt"] = (pk_sems[pslot].h, pk_sems[pslot].n)
        bslot = dqs["sbcslot"] % 3
        dqs["sbcslot"] += 1
        if u < 3:
            qe.wait_ge(s_scw.h, 32)
        if u >= 3:
            qe.wait_ge(s_dve.h, dqs.get("mdone_u%d" % (u - 3), 0))
            ws, wv = dqs["wwrval_u%d" % (u - 3)]
            qe.wait_ge(ws, wv)
        scr_sems[bslot].inc(qe.dma_start(
            out=sbc_sb[bslot][:, 0:ncol].rearrange("p (a c) -> p a c", a=1),
            in_=scsrc[:, col0:col0 + ncol].partition_broadcast(128)), 16)
        dqs["scr_evt"] = (scr_sems[bslot].h, scr_sems[bslot].n)
        v.wait_ge(*dqs["pk_evt"])
        if u >= 2:
            v.wait_ge(s_dve.h, dqs["mdone"][slot])
        pk16 = pk_sb[pslot][:, 0:nw].bitcast(mybir.dt.int16)
        nibv = nib_sb[slot][:, 0:8 * nw].rearrange("p (c h) -> p c h", h=8)
        for a in range(4):
            dst = nibv[:, :, 2 * a:2 * a + 2]
            s_dve.inc(v.tensor_scalar(
                dst, pk16[:, 0:2 * nw],
                4 * a, 15, OP.logical_shift_right, OP.bitwise_and))
        v.wait_ge(s_dve.h, s_dve.n)
        v.wait_ge(*dqs["scr_evt"])
        s_dve.inc(v.tensor_tensor(sbc_sb[bslot][:, 0:ncol], nib_sb[slot][:, 0:ncol],
                                  sbc_sb[bslot][:, 0:ncol], OP.mult))
        dqs["mdone"][slot] = s_dve.n
        dqs["mdone_u%d" % u] = s_dve.n
        dqs["pkf"][pslot] = s_dve.n
        qw = gp if which == "d" else sc
        qw.wait_ge(s_dve.h, s_dve.n)
        nblk = ncol // 256
        blk0 = col0 // 256
        dst = wdst[blk0:blk0 + nblk, kc].rearrange("b p c -> p b c")
        srcv = sbc_sb[bslot][:, 0:ncol].rearrange("p (b c) -> p b c", c=256)
        wwr_sems[bslot].inc(qw.dma_start(out=dst, in_=srcv), 16)
        dqs["wwrval_u%d" % u] = (wwr_sems[bslot].h, wwr_sems[bslot].n)
        if u == N_GU - 1:
            dqs["gu_wwr_totals"] = [s.n for s in s_wwr]

    def emit_dequant(n):
        i0 = dqs["i"]
        for i in range(i0, min(i0 + n, N_ALL)):
            dequant_unit(*DQ_UNITS[i])
        dqs["i"] = min(i0 + n, N_ALL)

    aph = ExitStack()
    AE = aph.enter_context
    ln1T = AE(nc.sbuf_tensor([128, 32 * GRP], BF16))
    PT = AE(nc.sbuf_tensor([128, 16 * GRP], BF16))
    x0_t = AE(nc.sbuf_tensor([128, D], F32))

    x0_loads = 0
    evac_done = [0, 0, 0, 0]
    fld_val = {}
    gat_n = 0
    gat_free = [0, 0, 0]
    rowb_used = [0]
    x1_writes = 0
    l3_writes = 0
    x0_free_evt = None
    last_evac = 0
    fpair_done = {}
    cuse_done = {}
    cstop_done = {}
    cread_done = {}

    ga = ExitStack()
    ln1b_t = ga.enter_context(nc.sbuf_tensor([128, D], BF16))
    for rt in range(NRT):
        r0 = rt * 128
        if x0_free_evt is None:
            sy.wait_ge(s_dve.h, P0_DONE)
        else:
            sy.wait_ge(x0_free_evt[0], x0_free_evt[1])
        s_x0.inc(sy.dma_start(out=x0_t[:], in_=x_in[r0:r0 + 128, :]), 16)
        x0_loads += 1
        v.wait_ge(s_x0.h, s_x0.n)
        ln_stats(x0_t)
        sc.wait_ge(s_dve.h, s_dve.n)
        if rt > 0:
            sc.wait_ge(s_zz.h, s_zz.n)
        ins = sc.activation(ln1b_t[:], x0_t[:], AF.Identity,
                            bias=nmrs_t[:], scale=rs_t[:])
        s_scl.inc(ins)
        x0_free_evt = (s_scl.h, s_scl.n)
        sc.wait_ge(s_scl.h, s_scl.n)
        s_zz.inc(sc.dma_start(out=zz_d[r0:r0 + 128, 0:D], in_=ln1b_t[:]), 16)
        s_zz.inc(sc.dma_start(out=zz_d[r0:r0 + 128, D:2 * D], in_=ln1b_t[:]), 16)
    ga.close()

    for g in range(NG):
        if g > 0:
            sy.wait_ge(s_dve.h, s_dve.n)
            sy.wait_ge(s_scl.h, s_scl.n)
            sy.wait_ge(s_x1w[0].h, s_x1w[0].n)
            sy.wait_ge(s_l3w.h, s_l3w.n)
        sy.wait_ge(s_zz.h, s_zz.n)
        for tc in range(32):
            s_l1t.inc(sy.dma_start(
                out=ln1T[:, tc * GRP:(tc + 1) * GRP],
                in_=zz_d[g * GRP:(g + 1) * GRP, tc * 128:(tc + 1) * 128],
                transpose=True), 16)

        ga = ExitStack()
        fcb = [ga.enter_context(nc.sbuf_tensor(f"fcb{g}_{i}", [128, 4096], BF16)) for i in range(3)]
        fsb = [ga.enter_context(nc.sbuf_tensor(f"fsb{g}_{i}", [128, 4096], BF16)) for i in range(3)]
        sq1 = ga.enter_context(nc.sbuf_tensor(f"sq1_{g}", [128, 512], F32))
        sq2 = ga.enter_context(nc.sbuf_tensor(f"sq2_{g}", [128, 512], F32))
        def emit_fload(fb2):
            if not (0 <= fb2 < 16):
                return
            slot2 = fb2 % 3
            pair2 = g * 16 + fb2
            if pair2 >= 3:
                sy.wait_ge(s_pe.h, fpair_done[pair2 - 3])
            s_fld[slot2].inc(sy.dma_start(out=fcb[slot2][:], in_=fc_d[fb2]), 16)
            s_fld[slot2].inc(sy.dma_start(out=fsb[slot2][:], in_=fs_d[fb2]), 16)
            fld_val[fb2] = (s_fld[slot2].h, s_fld[slot2].n)
        emit_fload(0)
        emit_fload(1)
        for fb in range(16):
            slot = fb % 3
            pair = g * 16 + fb
            emit_fload(fb + 2)
            bz_c, bz_s = 2 * (fb % 4), 2 * (fb % 4) + 1
            fh, fv = fld_val[fb]
            te.wait_ge(fh, fv)
            te.wait_ge(s_l1t.h, s_l1t.n)
            if pair >= 4:
                te.wait_ge(s_dve.h, evac_done[fb % 4])
            if g > 0 and fb < 8:
                te.wait_ge(s_dve.h, cread_done[(7, NRTG - 1)])
            for tc in range(32):
                te.matmul(ps[bz_c][:, 0:GRP], fcb[slot][:, tc * 128:(tc + 1) * 128],
                          ln1T[:, tc * GRP:(tc + 1) * GRP],
                          start=(tc == 0), stop=(tc == 31))
                mm = te.matmul(ps[bz_s][:, 0:GRP], fsb[slot][:, tc * 128:(tc + 1) * 128],
                               ln1T[:, tc * GRP:(tc + 1) * GRP],
                               start=(tc == 0), stop=(tc == 31))
                if tc == 31:
                    s_pe.inc(mm)
            fpair_done[pair] = s_pe.n
            sc.wait_ge(s_pe.h, s_pe.n)
            sc.wait_ge(s_dve.h, last_evac)
            s_scl.inc(sc.activation(sq1[:, 0:GRP], ps[bz_c][:, 0:GRP], AF.Square))
            s_scl.inc(sc.activation(sq2[:, 0:GRP], ps[bz_s][:, 0:GRP], AF.Square))
            v.wait_ge(s_scl.h, s_scl.n)
            s_dve.inc(v.tensor_tensor(PT[:, fb * GRP:(fb + 1) * GRP],
                                      sq1[:, 0:GRP], sq2[:, 0:GRP], OP.add))
            evac_done[fb % 4] = s_dve.n
            last_evac = s_dve.n
            emit_dequant(5)
        ga.close()
        pt_done = s_dve.n

        ga = ExitStack()
        ccs = [ga.enter_context(nc.sbuf_tensor(f"ccs{g}_{i}", [128, 512], BF16)) for i in range(2)]
        acc_t = ga.enter_context(nc.sbuf_tensor([128, D], F32))
        rolls = [ga.enter_context(nc.sbuf_tensor(f"roll{g}_{i}", [128, D], BF16)) for i in range(3)]
        x1_t = ga.enter_context(nc.sbuf_tensor([128, D], F32))
        ln3b_t = ga.enter_context(nc.sbuf_tensor([128, D], BF16))
        w8s = ga.enter_context(nc.sbuf_tensor([128, 8 * NRTG], F32))
        idxs = ga.enter_context(nc.sbuf_tensor([128, 8 * NRTG], I32))

        for fs_ in s_fld:
            sy.wait_ge(fs_.h, fs_.n)
        sy.wait_ge(s_pe.h, fpair_done[g * 16 + 15])
        for lc in range(8):
            for fb in range(16):
                cn = g * 128 + lc * 16 + fb
                cslot = cn % 2
                if cn >= 2:
                    sy.wait_ge(s_pe.h, cuse_done[cn - 2])
                s_cld[cslot].inc(sy.dma_start(
                    out=ccs[cslot][:],
                    in_=ccos_d[fb * 128:(fb + 1) * 128, lc * 512:(lc + 1) * 512]), 16)
                te.wait_ge(s_cld[cslot].h, s_cld[cslot].n)
                if fb == 0:
                    te.wait_ge(s_dve.h, pt_done)
                    if lc >= 1:
                        te.wait_ge(s_dve.h, cread_done[(lc - 1, NRTG - 1)])
                for rtl in range(NRTG):
                    last = te.matmul(
                        ps[rtl][:, 0:512],
                        PT[:, fb * GRP + rtl * 128:fb * GRP + (rtl + 1) * 128],
                        ccs[cslot][:], start=(fb == 0), stop=(fb == 15))
                    if fb == 15 or rtl == NRTG - 1:
                        s_pe.inc(last)
                    if fb == 15:
                        cstop_done[(lc, rtl)] = s_pe.n
                cuse_done[cn] = s_pe.n
            for rtl in range(NRTG):
                v.wait_ge(s_pe.h, cstop_done[(lc, rtl)])
                o8 = stkv[:, rtl * 64 + lc * 8: rtl * 64 + lc * 8 + 8]
                s_dve.inc(v.max(out=o8, in_=ps[rtl][:, 0:512]))
                v.wait_ge(s_dve.h, s_dve.n)
                s_dve.inc(v.max_index(out=tmpi8[:], in_max=o8,
                                      in_values=ps[rtl][:, 0:512]))
                v.wait_ge(s_dve.h, s_dve.n)
                ins = v.tensor_scalar(stki[:, rtl * 64 + lc * 8: rtl * 64 + lc * 8 + 8],
                                      tmpi8[:], float(512 * lc), None, OP.add)
                s_dve.inc(ins)
                cread_done[(lc, rtl)] = s_dve.n
            emit_dequant(8)

        idx_done = {}
        w8_done = {}
        gp.wait_ge(s_dve.h, rowb_used[0])
        for rtl in range(NRTG):
            s_gps.inc(gp.iota(rowb_i[:, rtl:rtl + 1], pattern=[[0, 1]],
                              base=(g * NRTG + rtl) * 128 * 2 * D,
                              channel_multiplier=2 * D))
        v.wait_ge(s_gps.h, s_gps.n)
        v.wait_ge(s_dve.h, s_dve.n)
        s_dve.inc(v.tensor_copy(rowb_f[:], rowb_i[:]))
        rowb_used[0] = s_dve.n
        for rtl in range(NRTG):
            rt = g * NRTG + rtl
            r0 = rt * 128
            sv = stkv[:, rtl * 64:(rtl + 1) * 64]
            si = stki[:, rtl * 64:(rtl + 1) * 64]
            v.wait_ge(s_dve.h, s_dve.n)
            v.wait_ge(s_scl.h, s_scl.n)
            s_dve.inc(v.max(out=g8[:], in_=sv))
            v.wait_ge(s_dve.h, s_dve.n)
            s_dve.inc(v.tensor_scalar_mul(negmax[:], g8[:, 0:1], -1.0))
            sc.wait_ge(s_dve.h, s_dve.n)
            sc.wait_ge(s_scl.h, s_scl.n)
            s_scl.inc(sc.activation(exp8[:], g8[:], AF.Exp, bias=negmax[:], scale=1.0))
            v.wait_ge(s_scl.h, s_scl.n)
            s_dve.inc(v.reduce_sum(sume[:], exp8[:], axis=AX.X))
            v.wait_ge(s_dve.h, s_dve.n)
            s_dve.inc(v.reciprocal(rsum[:], sume[:]))
            sc.wait_ge(s_dve.h, s_dve.n)
            s_scl.inc(sc.activation(w8s[:, rtl * 8:rtl * 8 + 8], exp8[:], AF.Identity,
                                    bias=0.0, scale=rsum[:]))
            w8_done[rtl] = s_scl.n
            for j in range(8):
                s_dve.inc(v.tensor_scalar(mask64[:], sv, g8[:, j:j + 1], None,
                                          OP.is_equal))
                v.wait_ge(s_dve.h, s_dve.n)
                s_dve.inc(v.tensor_mul(cand64[:], mask64[:], si))
                v.wait_ge(s_dve.h, s_dve.n)
                s_dve.inc(v.reduce_max(d8f[:, j:j + 1], cand64[:], axis=AX.X))
            v.wait_ge(s_dve.h, s_dve.n)
            s_dve.inc(v.tensor_scalar(idxf[:], d8f[:], rowb_f[:, rtl:rtl + 1],
                                      None, OP.add))
            v.wait_ge(s_dve.h, s_dve.n)
            s_dve.inc(v.tensor_copy(idxs[:, rtl * 8:rtl * 8 + 8], idxf[:]))
            idx_done[rtl] = s_dve.n

        for rtl in range(NRTG):
            rt = g * NRTG + rtl
            r0 = rt * 128
            sy.wait_ge(x0_free_evt[0], x0_free_evt[1])
            s_x0.inc(sy.dma_start(out=x0_t[:], in_=x_in[r0:r0 + 128, :]), 16)
            x0_loads += 1
            s_dve.inc(v.memset(acc_t[:], 0.0))
            gp.wait_ge(s_dve.h, idx_done[rtl])
            for k in range(8):
                gslot = gat_n % 3
                gp.wait_ge(s_dve.h, gat_free[gslot])
                s_gat[gslot].inc(gp.indirect_dma_start(
                    out=rolls[gslot][:], out_offset=None,
                    in_=zz_d[0:R, :],
                    in_offset=IndirectOffsetOnAxis(idxs[:, rtl * 8 + k:rtl * 8 + k + 1], 1)), 16)
                v.wait_ge(s_gat[gslot].h, s_gat[gslot].n)
                v.wait_ge(s_scl.h, w8_done[rtl])
                v.wait_ge(s_dve.h, s_dve.n)
                ins = v.scalar_tensor_tensor(acc_t[:], rolls[gslot][:],
                                             w8s[:, rtl * 8 + k:rtl * 8 + k + 1],
                                             acc_t[:], OP.mult, OP.add)
                s_dve.inc(ins)
                gat_free[gslot] = s_dve.n
                gat_n += 1
            ln_stats(acc_t)
            sc.wait_ge(s_dve.h, s_dve.n)
            if x1_writes > 0:
                sc.wait_ge(s_x1w[0].h, s_x1w[0].n)
            s_scl.inc(sc.activation(x1_t[:], acc_t[:], AF.Identity,
                                    bias=nmrs_t[:], scale=rs_t[:]))
            v.wait_ge(s_scl.h, s_scl.n)
            v.wait_ge(s_x0.h, s_x0.n)
            ins = v.tensor_tensor(x1_t[:], x1_t[:], x0_t[:], OP.add)
            s_dve.inc(ins)
            x0_free_evt = (s_dve.h, s_dve.n)
            ln_stats(x1_t)
            sc.wait_ge(s_dve.h, s_dve.n)
            if l3_writes > 0:
                sc.wait_ge(s_l3w.h, s_l3w.n)
            s_scl.inc(sc.activation(ln3b_t[:], x1_t[:], AF.Identity,
                                    bias=nmrs_t[:], scale=rs_t[:]))
            sc.wait_ge(s_scl.h, s_scl.n)
            s_x1w[0].inc(sc.dma_start(out=x1_d[r0:r0 + 128, :], in_=x1_t[:]), 16)
            x1_writes += 1
            s_l3w.inc(sc.dma_start(out=ln3_d[r0:r0 + 128, :], in_=ln3b_t[:]), 16)
            l3_writes += 1
            emit_dequant(4)
        ga.close()

    aph.close()

    bph = ExitStack()
    BE = bph.enter_context
    zs_g = BE(nc.sbuf_tensor([GK, IDIM], BF16))
    zs_u = BE(nc.sbuf_tensor([GK, IDIM], BF16))
    g32_sb = BE(nc.sbuf_tensor([128, GK * GK], BF16))
    sln3 = BE(nc.sbuf_tensor([GK, R], BF16))

    zb = ExitStack()
    znib = zb.enter_context(nc.sbuf_tensor([GI, IDIM], mybir.dt.int16))
    zq_gu = zb.enter_context(nc.sbuf_tensor([64, GUW], I32))
    zq_dn = zb.enter_context(nc.sbuf_tensor([GI, DW], I32))
    sy.wait_ge(s_x1w[0].h, s_x1w[0].n)
    sy.wait_ge(s_l3w.h, s_l3w.n)
    for dst, src in ((zq_gu[0:GK, :], gz), (zq_gu[GK:64, :], uz), (zq_dn[:], dz)):
        s_set.inc(sy.dma_start(out=dst, in_=src[:]), 16)
    v.wait_ge(s_set.h, s_set.n)
    v.wait_ge(s_x1w[0].h, s_x1w[0].n)
    v.wait_ge(s_l3w.h, s_l3w.n)
    zq16 = zq_gu[:].bitcast(mybir.dt.int16)
    for j in range(8):
        pos = POS_OF_SHIFT[j]
        s_dve.inc(v.tensor_scalar(znib[0:64, pos::8], zq16[:, j // 4::2],
                                  4 * (j % 4), 15,
                                  OP.logical_shift_right, OP.bitwise_and))
    v.wait_ge(s_dve.h, s_dve.n)
    s_dve.inc(v.scalar_tensor_tensor(zs_g[:], znib[0:GK, :], -1.0,
                                     sc_gu[0:GK, :], OP.mult, OP.mult))
    s_dve.inc(v.scalar_tensor_tensor(zs_u[:], znib[GK:64, :], -1.0,
                                     sc_gu[GK:64, :], OP.mult, OP.mult))
    v.wait_ge(s_dve.h, s_dve.n)
    zq16d = zq_dn[:].bitcast(mybir.dt.int16)
    for j in range(8):
        pos = POS_OF_SHIFT[j]
        s_dve.inc(v.tensor_scalar(znib[0:GI, pos:pos + 8 * (DW - 1) + 1:8],
                                  zq16d[:, j // 4::2], 4 * (j % 4), 15,
                                  OP.logical_shift_right, OP.bitwise_and))
    v.wait_ge(s_dve.h, s_dve.n)
    s_dve.inc(v.scalar_tensor_tensor(zs_dn[:], znib[0:GI, 0:D], -1.0,
                                     sc_dn[:], OP.mult, OP.mult))
    ZTAB_DONE = s_dve.n
    zb.close()
    ln3T = BE(nc.sbuf_tensor([128, 32 * R], BF16))
    wgub = [BE(nc.sbuf_tensor(f"wgub{i}", [128, 8 * 256], BF16)) for i in range(3)]
    silu_t = [BE(nc.sbuf_tensor(f"silu{i}", [128, 512], BF16)) for i in range(2)]
    h_t = [BE(nc.sbuf_tensor(f"ht{i}", [128, 512], BF16)) for i in range(2)]

    sy.wait_ge(s_l3w.h, s_l3w.n)
    sy.wait_ge(s_x1w[0].h, s_x1w[0].n)
    sy.wait_ge(s_dve.h, ZTAB_DONE)
    s_set.inc(sy.dma_start(out=g32_sb[:], in_=g32_d[:]), 16)
    for tc in range(32):
        s_l3t.inc(sy.dma_start(out=ln3T[:, tc * R:(tc + 1) * R],
                               in_=ln3_d[0:R, tc * 128:(tc + 1) * 128],
                               transpose=True), 16)
    te.wait_ge(s_l3t.h, s_l3t.n)
    te.wait_ge(s_dve.h, cread_done[(7, NRTG - 1)])
    te.wait_ge(s_set.h, s_set.n)
    for t in range(NTC):
        for kc in range(GK):
            mm = te.matmul(ps[6 + t][0:GK, 0:TCS],
                           g32_sb[:, kc * GK:(kc + 1) * GK],
                           ln3T[:, kc * R + t * TCS: kc * R + (t + 1) * TCS],
                           start=(kc == 0), stop=(kc == GK - 1))
            if kc == GK - 1:
                s_pe.inc(mm)
        v.wait_ge(s_pe.h, s_pe.n)
        s_dve.inc(v.tensor_copy(sln3[:, t * TCS:(t + 1) * TCS], ps[6 + t][0:GK, 0:TCS]))
    SLN3_DONE = s_dve.n

    emit_dequant(N_GU - dqs["i"] if dqs["i"] < N_GU else 0)
    for ws, wv in zip(s_wwr, dqs["gu_wwr_totals"]):
        te.wait_ge(ws.h, wv)
    te.wait_ge(s_dve.h, SLN3_DONE)
    hw_n = 0
    wq_n = 0
    wgu_done = {}
    bank_free = {}
    silu_used = [0, 0]
    ffnfree_scl = 0
    ffnfree_dve = 0
    for icb in range(43):
        for q in range(8):
            wslot = wq_n % 3
            if wq_n >= 3:
                sy.wait_ge(s_pe.h, wgu_done[wq_n - 3])
            if wq_n == 0:
                sy.wait_ge(s_l3w.h, s_l3w.n)
                for ws, wv in zip(s_wwr, dqs["gu_wwr_totals"]):
                    sy.wait_ge(ws.h, wv)
            w_i = q % 2
            kq = q // 2
            src = (wg_d if w_i == 0 else wu_d)[icb, kq * 8:(kq + 1) * 8]
            s_wgu[wslot].inc(sy.dma_start(
                out=wgub[wslot][:, 0:8 * 256].rearrange("p (k c) -> p k c", c=256),
                in_=src.rearrange("k p c -> p k c")), 16)
            wq_n += 1
            te.wait_ge(s_wgu[wslot].h, s_wgu[wslot].n)
            for kk in range(8):
                kc = kq * 8 + kk
                for ic2 in range(2):
                    lhs = wgub[wslot][:, kk * 256 + ic2 * 128: kk * 256 + (ic2 + 1) * 128]
                    for t in range(NTC):
                        bank = 4 * w_i + 2 * ic2 + t
                        if kc == 0 and icb >= 1:
                            fs, fv = bank_free[bank]
                            te.wait_ge(fs, fv)
                        last = te.matmul(ps[bank][:, 0:TCS], lhs,
                                  ln3T[:, kc * R + t * TCS: kc * R + (t + 1) * TCS],
                                  start=(kc == 0), stop=False)
            s_pe.inc(last)
            wgu_done[wq_n - 1] = s_pe.n
        for ic2 in range(2):
            ic = icb * 2 + ic2
            for t in range(NTC):
                bg = 2 * ic2 + t
                bu = 4 + 2 * ic2 + t
                mm = te.matmul(ps[bg][:, 0:TCS],
                               zs_g[:, ic * 128:(ic + 1) * 128],
                               sln3[:, t * TCS:(t + 1) * TCS],
                               start=False, stop=True)
                s_pe.inc(mm)
                zg = s_pe.n
                mm = te.matmul(ps[bu][:, 0:TCS],
                               zs_u[:, ic * 128:(ic + 1) * 128],
                               sln3[:, t * TCS:(t + 1) * TCS],
                               start=False, stop=True)
                s_pe.inc(mm)
                zu = s_pe.n
                ssl = hw_n % 2
                sc.wait_ge(s_pe.h, zg)
                if hw_n >= 2:
                    sc.wait_ge(s_dve.h, silu_used[ssl])
                ins = sc.activation(silu_t[ssl][:, 0:TCS], ps[bg][:, 0:TCS], AF.Silu)
                s_scl.inc(ins)
                bank_free[bg] = (s_scl.h, s_scl.n)
                hslot = hw_n % 2
                if hw_n >= 2:
                    v.wait_ge(s_hw[hslot].h, 16 * (hw_n // 2))
                v.wait_ge(s_scl.h, s_scl.n)
                v.wait_ge(s_pe.h, zu)
                ins = v.tensor_tensor(h_t[hslot][:, 0:TCS], silu_t[ssl][:, 0:TCS],
                                      ps[bu][:, 0:TCS], OP.mult)
                s_dve.inc(ins)
                silu_used[ssl] = s_dve.n
                bank_free[bu] = (s_dve.h, s_dve.n)
                gp.wait_ge(s_dve.h, s_dve.n)
                s_hw[hslot].inc(gp.dma_start(
                    out=h_d[ic, :, t * TCS:(t + 1) * TCS],
                    in_=h_t[hslot][:, 0:TCS]), 16)
                hw_n += 1
        ffnfree_scl = s_scl.n
        ffnfree_dve = s_dve.n
        emit_dequant(4)
    emit_dequant(N_ALL)
    bph.close()
    s1.close()

    dph = ExitStack()
    DE = dph.enter_context
    h_sb = DE(nc.sbuf_tensor([128, GI * R], BF16))
    gph = ExitStack()
    g86_sb = gph.enter_context(nc.sbuf_tensor([128, GI * GI], BF16))

    sy.wait_ge(s_scl.h, ffnfree_scl)
    sy.wait_ge(s_dve.h, ffnfree_dve)
    sy.wait_ge(s_hw[0].h, s_hw[0].n)
    sy.wait_ge(s_hw[1].h, s_hw[1].n)
    for ws in s_wwr + s_wwrg:
        sy.wait_ge(ws.h, ws.n)
    s_set.inc(sy.dma_start(out=g86_sb[:], in_=g86_d[:]), 16)
    for ic in range(GI):
        s_hld.inc(sy.dma_start(out=h_sb[:, ic * R:(ic + 1) * R], in_=h_d[ic]), 16)
    te.wait_ge(s_hld.h, s_hld.n)
    te.wait_ge(s_set.h, s_set.n)
    for t in range(NTC):
        for ic in range(GI):
            mm = te.matmul(ps[t][0:GI, 0:TCS], g86_sb[:, ic * GI:(ic + 1) * GI],
                           h_sb[:, ic * R + t * TCS: ic * R + (t + 1) * TCS],
                           start=(ic == 0), stop=(ic == GI - 1))
            if ic == GI - 1:
                s_pe.inc(mm)
        v.wait_ge(s_pe.h, s_pe.n)
        s_dve.inc(v.tensor_copy(hg_sb[:, t * TCS:(t + 1) * TCS], ps[t][0:GI, 0:TCS]))
    HG_DONE = s_dve.n
    gph.close()

    wd_sb = [DE(nc.sbuf_tensor(f"wdsb{i}", [128, 8 * 256], BF16)) for i in range(3)]
    fT_t = [DE(nc.sbuf_tensor(f"fTt{i}", [128, 512], BF16)) for i in range(2)]
    SLABS = []
    for dchb in range(16):
        i0 = 0
        while i0 < GI:
            nic = min(8, GI - i0)
            SLABS.append((dchb, i0, nic))
            i0 += nic
    fT_n = 0
    wd_done = {}
    dfree_done = {}
    for si, (dchb, i0, nic) in enumerate(SLABS):
        wsl = si % 3
        if si >= 3:
            sy.wait_ge(s_pe.h, wd_done[si - 3])
        if si == 0:
            sy.wait_ge(s_dve.h, HG_DONE)
        s_wdl[wsl].inc(sy.dma_start(
            out=wd_sb[wsl][:, 0:nic * 256].rearrange("p (i c) -> p i c", c=256),
            in_=wd_d[dchb, i0:i0 + nic].rearrange("i p c -> p i c")), 16)
        te.wait_ge(s_wdl[wsl].h, s_wdl[wsl].n)
        if i0 == 0:
            if dchb == 0:
                te.wait_ge(s_dve.h, HG_DONE)
            if dchb >= 2:
                te.wait_ge(s_scl.h, dfree_done[dchb - 2])
        last = None
        for ii in range(nic):
            ic = i0 + ii
            for dch2 in range(2):
                lhs = wd_sb[wsl][:, ii * 256 + dch2 * 128: ii * 256 + (dch2 + 1) * 128]
                for t in range(NTC):
                    bank = 4 * (dchb % 2) + 2 * dch2 + t
                    last = te.matmul(ps[bank][:, 0:TCS], lhs,
                                     h_sb[:, ic * R + t * TCS: ic * R + (t + 1) * TCS],
                                     start=(ic == 0), stop=False)
        s_pe.inc(last)
        wd_done[si] = s_pe.n
        if i0 + nic == GI:
            for dch2 in range(2):
                dch = 2 * dchb + dch2
                for t in range(NTC):
                    bank = 4 * (dchb % 2) + 2 * dch2 + t
                    mm = te.matmul(ps[bank][:, 0:TCS],
                                   zs_dn[:, dch * 128:(dch + 1) * 128],
                                   hg_sb[:, t * TCS:(t + 1) * TCS],
                                   start=False, stop=True)
                    s_pe.inc(mm)
            pe_done = s_pe.n
            for dch2 in range(2):
                dch = 2 * dchb + dch2
                for t in range(NTC):
                    bank = 4 * (dchb % 2) + 2 * dch2 + t
                    fsl = fT_n % 2
                    if fT_n >= 2:
                        sc.wait_ge(s_fT[fsl].h, 16 * (fT_n // 2))
                    sc.wait_ge(s_pe.h, pe_done)
                    ins = sc.activation(fT_t[fsl][:, 0:TCS], ps[bank][:, 0:TCS],
                                        AF.Identity, bias=0.0, scale=1.0)
                    s_scl.inc(ins)
                    sc.wait_ge(s_scl.h, s_scl.n)
                    s_fT[fsl].inc(sc.dma_start(
                        out=ffnT_d[dch * 128:(dch + 1) * 128, t * TCS:(t + 1) * TCS],
                        in_=fT_t[fsl][:, 0:TCS]), 16)
                    fT_n += 1
            dfree_done[dchb] = s_scl.n
    dph.close()

    pph = ExitStack()
    PE_ = pph.enter_context
    ftok = [PE_(nc.sbuf_tensor(f"ftok{i}", [128, D], BF16)) for i in range(2)]
    x1l_t = [PE_(nc.sbuf_tensor(f"x1l{i}", [128, D], F32)) for i in range(2)]
    ln4n_t = PE_(nc.sbuf_tensor([128, D], F32))
    out_t = [PE_(nc.sbuf_tensor(f"outt{i}", [128, D], F32)) for i in range(2)]

    sy.wait_ge(s_fT[0].h, s_fT[0].n)
    sy.wait_ge(s_fT[1].h, s_fT[1].n)
    sy.wait_ge(s_x1w[0].h, s_x1w[0].n)
    p4_free = {}
    for rt in range(NRT):
        r0 = rt * 128
        sl = rt % 2
        if rt >= 2:
            sy.wait_ge(s_dve.h, p4_free[rt - 2])
        s_fTl[sl].inc(sy.dma_start(out=ftok[sl][:], in_=ffnT_d[0:D, r0:r0 + 128],
                                   transpose=True), 16)
        s_x1l[sl].inc(sy.dma_start(out=x1l_t[sl][:], in_=x1_d[r0:r0 + 128, :]), 16)
        v.wait_ge(s_fTl[sl].h, s_fTl[sl].n)
        ln_stats(ftok[sl])
        sc.wait_ge(s_dve.h, s_dve.n)
        s_scl.inc(sc.activation(ln4n_t[:], ftok[sl][:], AF.Identity,
                                bias=nmrs_t[:], scale=rs_t[:]))
        v.wait_ge(s_scl.h, s_scl.n)
        v.wait_ge(s_x1l[sl].h, s_x1l[sl].n)
        if rt >= 2:
            v.wait_ge(s_out[sl].h, 16 * (rt // 2))
        ins = v.tensor_add(out_t[sl][:], ln4n_t[:], x1l_t[sl][:])
        s_dve.inc(ins)
        p4_free[rt] = s_dve.n
        sy.wait_ge(s_dve.h, s_dve.n)
        s_out[sl].inc(sy.dma_start(out=out_d[r0:r0 + 128, :], in_=out_t[sl][:]), 16)
    sy.wait_ge(s_out[0].h, s_out[0].n)
    sy.wait_ge(s_out[1].h, s_out[1].n)

    pph.close()
    ctx.close()
    return nc


_PROG_CACHE = {}


def _get_prog(R):
    if R not in _PROG_CACHE:
        _PROG_CACHE[R] = build_program(R)
    return _PROG_CACHE[R]


def kernel(L0_attn_x, L0_ffn_gate_qweight, L0_ffn_gate_scales, L0_ffn_gate_qzeros,
           L0_ffn_up_qweight, L0_ffn_up_scales, L0_ffn_up_qzeros,
           L0_ffn_down_qweight, L0_ffn_down_scales, L0_ffn_down_qzeros):
    x = np.ascontiguousarray(np.asarray(L0_attn_x, dtype=np.float32)).reshape(8192, D)
    R = 8192 // 8
    cst = _consts()
    nc = _get_prog(R)
    shared = dict(
        gq=np.ascontiguousarray(np.asarray(L0_ffn_gate_qweight, dtype=np.int32)),
        gs=np.ascontiguousarray(np.asarray(L0_ffn_gate_scales, dtype=np.float32)),
        gz=np.ascontiguousarray(np.asarray(L0_ffn_gate_qzeros, dtype=np.int32)),
        uq=np.ascontiguousarray(np.asarray(L0_ffn_up_qweight, dtype=np.int32)),
        us=np.ascontiguousarray(np.asarray(L0_ffn_up_scales, dtype=np.float32)),
        uz=np.ascontiguousarray(np.asarray(L0_ffn_up_qzeros, dtype=np.int32)),
        dq=np.ascontiguousarray(np.asarray(L0_ffn_down_qweight, dtype=np.int32)),
        ds=np.ascontiguousarray(np.asarray(L0_ffn_down_scales, dtype=np.float32)),
        dz=np.ascontiguousarray(np.asarray(L0_ffn_down_qzeros, dtype=np.int32)),
        fc4=cst["fc4"], fs4=cst["fs4"], ccos=cst["ccos"],
        g32=cst["g32"], g86=cst["g86"])
    in_maps = [dict(shared, x=np.ascontiguousarray(x[c * R:(c + 1) * R]))
               for c in range(8)]
    res = run_bass_kernel_spmd(nc, in_maps, core_ids=list(range(8)))
    out = np.concatenate([np.asarray(r["out"], dtype=np.float32)
                          for r in res.results], axis=0)
    return out.reshape(4, 2048, D)
